# version 27
# baseline (speedup 1.0000x reference)
import math
import os
import contextlib
import numpy as np
import concourse.bass as bass
import concourse.mybir as mybir
from concourse.bass_utils import run_bass_kernel_spmd
from concourse.alu_op_type import AluOpType

F32 = mybir.dt.float32
BF16 = mybir.dt.bfloat16
AF = mybir.ActivationFunctionType
AX = mybir.AxisListType.X
MUL = AluOpType.mult
ADD = AluOpType.add
SUB = AluOpType.subtract
GT = AluOpType.is_gt
LT = AluOpType.is_lt
MAXOP = AluOpType.max

D = 2048
T = 4096
H = 16
DR = 64
DQK = 192
LQ = 1536
LKV = 512
NT = 32
NQT = 8
SCALE = 1.0 / math.sqrt(DQK)
EPS_LN = 1e-5
EPS_RMS = 1e-6
NEG_BIG = -3.0e38

(SUM, SQ, SQ2, SQ3, MEAN, M2, VAR, SD, RSTD, RC, TCOL, CCOL, NTCOL,
 MX, BIAS, SA, SK, SSUM, RS, ACC, TMP) = range(21)


class KB:
    ENG_SEM = {"tensor": "te", "vector": "ve", "scalar": "se"}
    Q_SEMS = {"sync": ["s0", "s1", "s2", "s3"], "scalar": ["t0", "t1", "t2", "t3"],
              "gpsimd": ["g0", "g1", "g2", "g3"]}
    ALL_SEMS = ("te", "ve", "se", "gc", "outd", "s0", "s1", "s2", "s3",
                "t0", "t1", "t2", "t3", "g0", "g1", "g2", "g3")

    def __init__(self):
        self.ops = {k: [] for k in ("tensor", "vector", "scalar", "sync", "gpsimd")}
        self.cnt = {k: 0 for k in self.ALL_SEMS}
        self.rr = {k: 0 for k in self.Q_SEMS}
        self.lastw = {k: {} for k in self.ops}
        self.sems = {}

    def _wait(self, engine, waits):
        for w in waits:
            if w is None:
                continue
            s, v = w
            if v <= 0 or self.lastw[engine].get(s, 0) >= v:
                continue
            self.lastw[engine][s] = v

            def f(eng, _s=s, _v=v):
                eng.wait_ge(self.sems[_s], _v)
            self.ops[engine].append(f)

    def op(self, engine, fn, waits=(), relax=False):
        s = self.ENG_SEM[engine]
        if engine in ("vector", "scalar") and not relax:
            self._wait(engine, list(waits) + [(s, self.cnt[s])])
        else:
            self._wait(engine, waits)
        self.cnt[s] += 1
        tok = self.cnt[s]

        def f(eng, _fn=fn, _s=s):
            _fn(eng).then_inc(self.sems[_s], 1)
        self.ops[engine].append(f)
        return (s, tok)

    def dma(self, queue, out, in_, waits=(), sem=None, transpose=False):
        if sem is None:
            pool = self.Q_SEMS[queue]
            s = pool[self.rr[queue] % len(pool)]
            self.rr[queue] += 1
        else:
            s = sem
        prev = self.cnt[s]
        self._wait(queue, list(waits) + [(s, prev)])
        self.cnt[s] = prev + 16
        tok = self.cnt[s]

        def f(eng, _o=out, _i=in_, _s=s, _t=transpose):
            eng.dma_start(out=_o, in_=_i, transpose=_t).then_inc(self.sems[_s], 16)
        self.ops[queue].append(f)
        return (s, tok)

    def raw(self, engine, fn, waits=()):
        self._wait(engine, waits)
        self.ops[engine].append(fn)

    def now(self, *sems):
        return [(s, self.cnt[s]) for s in sems]

    def now_all(self):
        return [(s, self.cnt[s]) for s in self.ALL_SEMS if self.cnt[s] > 0]


def _j_first(j):
    return j == 0


def set_tokens(a, b):
    return (a or []) + (b or [])


def _j_set(j):
    return j


def build_program(upto="full"):
    nc = bass.Bass("TRN2", target_bir_lowering=False)
    kb = KB()

    x0_d = nc.dram_tensor("x0", [NQT * 128, D], F32, kind="ExternalInput").ap()
    wqa_d = nc.dram_tensor("wqa", [D, LQ], F32, kind="ExternalInput").ap()
    qnw_d = nc.dram_tensor("qnw", [1, LQ], F32, kind="ExternalInput").ap()
    wqb_d = nc.dram_tensor("wqb", [LQ, H * DQK], F32, kind="ExternalInput").ap()
    wkr_d = nc.dram_tensor("wkr", [D, DR], F32, kind="ExternalInput").ap()
    wkva_d = nc.dram_tensor("wkva", [D, LKV], F32, kind="ExternalInput").ap()
    kvnw_d = nc.dram_tensor("kvnw", [1, LKV], F32, kind="ExternalInput").ap()
    wkvb_d = nc.dram_tensor("wkvb", [LKV, H * 256], F32, kind="ExternalInput").ap()
    wo_d = nc.dram_tensor("wo", [D, D], F32, kind="ExternalInput").ap()
    fw_d = nc.dram_tensor("fw", [D, D], F32, kind="ExternalInput").ap()
    fb_d = nc.dram_tensor("fb", [1, D], F32, kind="ExternalInput").ap()
    eye_d = nc.dram_tensor("eye", [128, 128], F32, kind="ExternalInput").ap()
    mask_d = nc.dram_tensor("maskc", [128, 512], F32, kind="ExternalInput").ap()
    cqt_d = nc.dram_tensor("cqt", [32, 1024], F32, kind="ExternalInput").ap()
    sqt_d = nc.dram_tensor("sqt", [32, 1024], F32, kind="ExternalInput").ap()
    y_d = nc.dram_tensor("y", [NQT * 128, D], F32, kind="ExternalOutput").ap()
    x1s_d = nc.dram_tensor("x1s", [NQT * 128, D], F32, kind="Internal").ap()
    cex_i = nc.dram_tensor("cex_i", [128, 4 * 1024], BF16, kind="Internal").ap()
    cex_o = nc.dram_tensor("cex_o", [4 * 128, 4 * 1024], BF16, kind="Internal").ap()
    krx_i = nc.dram_tensor("krx_i", [64, 1024], BF16, kind="Internal").ap()
    krx_o = nc.dram_tensor("krx_o", [4 * 64, 1024], BF16, kind="Internal").ap()
    x2s_d = nc.dram_tensor("x2s", [NQT * 128, D], F32, kind="Internal").ap()
    wts_d = nc.dram_tensor("wts", [D, D], BF16, kind="Internal").ap()
    o_d = nc.dram_tensor("o_scr", [128, H * 1024], BF16, kind="Internal").ap()

    glob = contextlib.ExitStack()

    def sbg(name, shape, dt):
        return glob.enter_context(nc.sbuf_tensor(name, shape, dt))

    cols = sbg("cols", [128, 32], F32)
    rows = sbg("rows", [1, 512], F32)
    m_cols = sbg("m_cols", [128, 8], F32)
    ones_f = sbg("ones_f", [1, 128], F32)
    ones_c = sbg("ones_c", [128, 1], F32)
    id_b = sbg("id_b", [128, 128], BF16)
    hb = sbg("hb", [128, D], BF16)

    def C(i, n=1):
        return cols[:, i:i + n]

    psS1 = glob.enter_context(nc.psum_tensor("psS1", [128, 1024], F32))
    psS2 = glob.enter_context(nc.psum_tensor("psS2", [128, 1024], F32))
    psA = psS1[:, 0:512]
    psB = psS1[:, 512:1024]
    psC = glob.enter_context(nc.psum_tensor("psC", [128, 512], F32))
    psD = glob.enter_context(nc.psum_tensor("psD", [128, 512], F32))
    psE = glob.enter_context(nc.psum_tensor("psE", [128, 512], F32))
    psF = glob.enter_context(nc.psum_tensor("psF", [128, 512], F32))
    ps_pool = [psA, psB]
    ps_o = [psC, psD]
    ps_gate = {0: [], 1: []}

    t_id = kb.dma("gpsimd", id_b[:, :], eye_d)
    t_m1 = kb.op("vector", lambda e: e.memset(ones_f[:, :], 1.0))
    t_m2 = kb.op("vector", lambda e: e.memset(ones_c[:, :], 1.0))
    t_m3 = kb.op("vector", lambda e: e.memset(C(ACC), 0.0))

    stA = contextlib.ExitStack()
    w_tT = stA.enter_context(nc.sbuf_tensor("w_tT", [128, 16 * D], BF16))
    p0 = contextlib.ExitStack()
    wtile = [p0.enter_context(nc.sbuf_tensor(f"wtile{i}", [128, D], F32))
             for i in range(4)]
    gtl2 = [p0.enter_context(nc.sbuf_tensor(f"gtl{i}", [128, D], F32))
            for i in range(2)]
    wt_tm = [p0.enter_context(nc.sbuf_tensor(f"wt_tm{i}", [128, D], BF16))
             for i in range(2)]

    rd_tok = {}
    for oc in range(16):
        w = wtile[oc % 4]
        ld = kb.dma("scalar", w[:, :], fw_d[oc * 128:(oc + 1) * 128, :],
                    waits=[rd_tok.get(oc - 4)])
        kb.op("vector", lambda e, _w=w: e.reduce_sum(
            out=C(TMP), in_=_w[:, :], axis=AX, apply_absolute_value=True), waits=[ld])
        rd_tok[oc] = kb.op("vector", lambda e: e.tensor_tensor(
            out=C(ACC), in0=C(ACC), in1=C(TMP), op=ADD), waits=[t_m3])
    t_tot = kb.op("tensor", lambda e: e.matmul(
        psE[0:1, 0:1], C(ACC), ones_c[:, :], start=True, stop=True),
        waits=[rd_tok[15], t_m2, t_m1])
    kb.op("vector", lambda e: e.tensor_scalar(
        out=rows[0:1, 0:1], in0=psE[0:1, 0:1], scalar1=0.5 / float(D * D),
        scalar2=0.5e-10, op0=MUL, op1=ADD), waits=[t_tot])
    t_rw = kb.op("vector", lambda e: e.tensor_scalar(
        out=rows[0:1, 1:2], in0=psE[0:1, 0:1], scalar1=0.5 / float(D * D),
        scalar2=None, op0=MUL))
    t_bc = kb.op("tensor", lambda e: e.matmul(
        psE[:, 2:4], ones_f[:, :], rows[0:1, 0:2], start=True, stop=True),
        waits=[t_rw, t_m1])
    kb.op("vector", lambda e: e.tensor_copy(out=C(TCOL), in_=psE[:, 2:3]), waits=[t_bc])
    t_cc = kb.op("vector", lambda e: e.tensor_copy(out=C(CCOL), in_=psE[:, 3:4]))
    t_nt = kb.op("vector", lambda e: e.tensor_scalar(
        out=C(NTCOL), in0=C(TCOL), scalar1=-1.0, scalar2=None, op0=MUL))

    qz_tok = {}
    trl_tok = {}
    cp_hist = []
    for oc in range(16):
        w = wtile[oc % 4]
        ld = kb.dma("scalar", w[:, :], fw_d[oc * 128:(oc + 1) * 128, :],
                    waits=[qz_tok.get(oc - 4, t_nt)])
        wt = wt_tm[oc % 2]
        gtl = gtl2[oc % 2]
        kb.op("vector", lambda e, _w=w, _g=gtl: e.tensor_scalar(
            out=_g[:, :], in0=_w[:, :], scalar1=C(NTCOL), scalar2=None, op0=LT),
            waits=[ld, t_nt, qz_tok.get(oc - 2)], relax=True)
        tq2 = kb.op("vector", lambda e, _w=w, _wt=wt, _g=gtl: e.scalar_tensor_tensor(
            out=_wt[:, :], in0=_w[:, :], scalar=C(TCOL), in1=_g[:, :], op0=GT,
            op1=SUB), waits=[trl_tok.get(oc - 2)])
        qz_tok[oc] = tq2
        trl_tok[oc] = kb.dma(
            "scalar",
            w_tT[:, :].rearrange("p (ic f) -> p ic f", ic=16)[:, :,
                                                              oc * 128:(oc + 1) * 128],
            wt[:, :].rearrange("p (ic c) -> p ic c", ic=16),
            waits=[tq2], transpose=True)
    wtT_snap = kb.now("t0", "t1", "t2", "t3")
    t_wtT_ready = wtT_snap[0]
    for ic in range(16):
        kb.dma("gpsimd", wts_d[ic * 128:(ic + 1) * 128, :],
               w_tT[:, ic * D:(ic + 1) * D], waits=wtT_snap)

    prev = None
    for g in range(4):
        ldb = kb.dma("sync", rows[0:1, 0:512], fb_d[0:1, g * 512:(g + 1) * 512],
                     waits=[prev] + ([t_bc, t_rw] if g == 0 else []))
        th = kb.op("vector", lambda e: e.tensor_scalar(
            out=rows[0:1, 0:512], in0=rows[0:1, 0:512], scalar1=0.5, scalar2=None,
            op0=MUL), waits=[ldb])
        tb = kb.op("tensor", lambda e: e.matmul(
            psE[:, 0:512], ones_f[:, :], rows[0:1, 0:512], start=True, stop=True),
            waits=[th])
        prev = kb.op("vector", lambda e, _g=g: e.tensor_copy(
            out=hb[:, _g * 512:(_g + 1) * 512], in_=psE[:, 0:512]), waits=[tb])
    t_hb = prev

    def ffn_phase(stk, name, src_fn, src_tok, dst_fn, ntiles, out_sem, wtT_buf,
                  ready, load_ready=()):
        x0t = [stk.enter_context(nc.sbuf_tensor(f"{name}x{i}", [128, D], F32))
               for i in range(2)]
        stage2 = [stk.enter_context(nc.sbuf_tensor(f"{name}st{i}", [128, D], F32))
                  for i in range(2)]
        z_tm2 = [stk.enter_context(nc.sbuf_tensor(f"{name}zt{i}", [128, D], BF16))
                 for i in range(2)]
        z_fm2 = [stk.enter_context(nc.sbuf_tensor(f"{name}zf{i}", [128, 16 * 128],
                                                  BF16)) for i in range(2)]
        x1t2 = [stk.enter_context(nc.sbuf_tensor(f"{name}o{i}", [128, D], F32))
                for i in range(2)]
        last_res = {}
        last_sq = {}
        last_store = {}
        last_mm = {}
        last_tr = {}
        stores = []
        for t in range(ntiles):
            xb = x0t[t % 2]
            stage = stage2[t % 2]
            z_tm = z_tm2[t % 2]
            z_fm = z_fm2[t % 2]
            x1t = x1t2[t % 2]
            lw = [last_res.get(t - 2), last_sq.get(t - 2), src_tok(t)]
            if t < 2:
                lw += list(load_ready)
            ld = kb.dma("sync", xb[:, :], src_fn(t), waits=lw)
            kb.op("vector", lambda e, _x=xb: e.reduce_sum(
                out=C(SUM), in_=_x[:, :], axis=AX), waits=[ld])
            ts_sq = kb.op("scalar", lambda e, _x=xb, _st=stage: e.activation(
                out=_st[:, :], in_=_x[:, :], func=AF.Square, accum_out=C(SQ)),
                waits=[ld, last_res.get(t - 2)])
            last_sq[t] = ts_sq
            kb.op("vector", lambda e: e.tensor_scalar(
                out=C(MEAN), in0=C(SUM), scalar1=1.0 / D, scalar2=None, op0=MUL))
            kb.op("vector", lambda e: e.tensor_scalar(
                out=C(M2), in0=C(MEAN), scalar1=C(MEAN), scalar2=None, op0=MUL))
            kb.op("vector", lambda e: e.tensor_scalar(
                out=C(VAR), in0=C(SQ), scalar1=1.0 / D, scalar2=EPS_LN,
                op0=MUL, op1=ADD), waits=[ts_sq])
            t_var = kb.op("vector", lambda e: e.tensor_tensor(
                out=C(VAR), in0=C(VAR), in1=C(M2), op=SUB))
            t_sd = kb.op("scalar", lambda e: e.activation(
                out=C(SD), in_=C(VAR), func=AF.Sqrt), waits=[t_var])
            kb.op("vector", lambda e: e.reciprocal(out=C(RSTD), in_=C(SD)), waits=[t_sd])
            t_rc = kb.op("vector", lambda e: e.tensor_scalar(
                out=C(RC), in0=C(RSTD), scalar1=C(CCOL), scalar2=None, op0=MUL))
            kb.op("vector", lambda e, _x=xb, _st=stage: e.tensor_scalar(
                out=_st[:, :], in0=_x[:, :], scalar1=C(MEAN), scalar2=None, op0=SUB),
                waits=[ts_sq])
            t_z = kb.op("vector", lambda e, _zt=z_tm, _st=stage: e.tensor_scalar(
                out=_zt[:, :], in0=_st[:, :], scalar1=C(RC), scalar2=None, op0=MUL),
                waits=[t_rc] + list(last_tr.get(t - 2) or []))
            t_ztr = kb.dma("scalar",
                           z_fm[:, :].rearrange("p (t c) -> p t c", t=16),
                           z_tm[:, :].rearrange("p (t c) -> p t c", t=16),
                           waits=[t_z, last_mm.get(t - 2)], transpose=True)
            zfm_snap = [t_ztr]
            last_tr[t] = zfm_snap
            for og in range(4):
                par = og % 2
                ps = ps_pool[par]
                mw = list(zfm_snap) + list(ready) + ps_gate[par]
                tok_mm = None
                for ic in range(16):
                    tok_mm = kb.op("tensor", lambda e, _ic=ic, _og=og, _ps=ps,
                                   _zf=z_fm: e.matmul(
                        _ps[:, :], _zf[:, _ic * 128:(_ic + 1) * 128],
                        wtT_buf[:, _ic * D + _og * 512: _ic * D + (_og + 1) * 512],
                        start=(_ic == 0), stop=(_ic == 15)),
                        waits=mw if ic == 0 else [])
                last_mm[t] = tok_mm
                ew = [tok_mm, t_hb]
                if og == 0:
                    ew.append(last_store.get(t - 2))
                t_e1 = kb.op("vector", lambda e, _og=og, _ps=ps,
                             _st=stage: e.scalar_tensor_tensor(
                    out=_st[:, _og * 512:(_og + 1) * 512], in0=_ps[:, :], scalar=1.0,
                    in1=hb[:, _og * 512:(_og + 1) * 512], op0=MUL, op1=ADD), waits=ew,
                    relax=(og > 0))
                ps_gate[par] = [t_e1]
                t_e2 = kb.op("vector", lambda e, _og=og, _x=xb, _st=stage,
                             _xo=x1t: e.tensor_tensor(
                    out=_xo[:, _og * 512:(_og + 1) * 512],
                    in0=_st[:, _og * 512:(_og + 1) * 512],
                    in1=_x[:, _og * 512:(_og + 1) * 512], op=ADD))
                last_res[t] = t_e2
            st = kb.dma("scalar", dst_fn(t), x1t[:, :], waits=[last_res[t]],
                        sem=out_sem)
            last_store[t] = st
            stores.append(st)
        return stores

    if upto == "p0":
        dbg = kb.dma("gpsimd", y_d[0:128, :], hb[:, :], waits=[t_hb, t_wtT_ready],
                     sem="outd")
        kb.raw("sync", lambda e: e.wait_ge(kb.sems["outd"], 16))
        return _assemble(nc, kb, glob)
    p1a = contextlib.ExitStack()
    p1a_stores = ffn_phase(
        p1a, "fa", lambda t: x0_d[t * 128:(t + 1) * 128, :], lambda t: None,
        lambda t: x1s_d[t * 128:(t + 1) * 128, :], NQT, None,
        w_tT, wtT_snap + [t_cc, t_hb])
    bar1 = kb.now_all()
    if upto == "p1a":
        kb.dma("sync", y_d[0:128, :], x1s_d[0:128, :], waits=bar1, sem="outd")
        kb.raw("sync", lambda e: e.wait_ge(kb.sems["outd"], 16))
        return _assemble(nc, kb, glob)
    p1a.close()
    p0.close()
    stA.close()

    res = contextlib.ExitStack()
    ckv_fm = res.enter_context(nc.sbuf_tensor("ckv_fm", [128, 4 * T], BF16))
    krot_fm = res.enter_context(nc.sbuf_tensor("krot_fm", [128, T], BF16))
    cq_fm = res.enter_context(nc.sbuf_tensor("cq_fm", [128, 12 * 1024], BF16))

    p1b = contextlib.ExitStack()
    wqa_s = p1b.enter_context(nc.sbuf_tensor("wqa_s", [128, 16 * LQ], BF16))
    wkva_s = p1b.enter_context(nc.sbuf_tensor("wkva_s", [128, 16 * LKV], BF16))
    wkr_s = p1b.enter_context(nc.sbuf_tensor("wkr_s", [128, 16 * DR], BF16))
    x1r = [p1b.enter_context(nc.sbuf_tensor(f"x1r{i}", [128, D], F32))
           for i in range(2)]
    stage1 = p1b.enter_context(nc.sbuf_tensor("stage1", [128, D], F32))
    h_tm2 = [p1b.enter_context(nc.sbuf_tensor(f"h_tm{i}", [128, D], BF16))
             for i in range(2)]
    h_fm2 = [p1b.enter_context(nc.sbuf_tensor(f"h_fm{i}", [128, 16 * 128], BF16))
             for i in range(2)]
    cqs = p1b.enter_context(nc.sbuf_tensor("cqs", [128, LQ], F32))
    cqn_tm = p1b.enter_context(nc.sbuf_tensor("cqn_tm", [128, LQ], BF16))
    ckvn_tm = p1b.enter_context(nc.sbuf_tensor("ckvn_tm", [128, LKV], BF16))
    wq_t = p1b.enter_context(nc.sbuf_tensor("wq_t", [128, LQ], F32))
    wkv_t = p1b.enter_context(nc.sbuf_tensor("wkv_t", [128, LKV], F32))
    cksk = p1b.enter_context(nc.sbuf_tensor("cksk", [64, 1024], BF16))
    ckv_own = p1b.enter_context(nc.sbuf_tensor("ckv_own", [128, 4 * 1024], BF16))
    krot_own = p1b.enter_context(nc.sbuf_tensor("krot_own", [64, 1024], BF16))

    wl_last = None
    for c0 in range(16):
        kb.dma("gpsimd", wqa_s[:, c0 * LQ:(c0 + 1) * LQ],
               wqa_d[c0 * 128:(c0 + 1) * 128, :], waits=bar1 if c0 == 0 else [])
        kb.dma("gpsimd", wkva_s[:, c0 * LKV:(c0 + 1) * LKV],
               wkva_d[c0 * 128:(c0 + 1) * 128, :])
        wl_last = kb.dma("gpsimd", wkr_s[:, c0 * DR:(c0 + 1) * DR],
                         wkr_d[c0 * 128:(c0 + 1) * 128, :])
    kb.dma("gpsimd", cksk[0:32, :], cqt_d)
    kb.dma("gpsimd", cksk[32:64, :], sqt_d)
    wl_toks = kb.now("g0", "g1", "g2", "g3")

    prev = None
    for g in range(3):
        ldq = kb.dma("sync", rows[0:1, 0:512], qnw_d[0:1, g * 512:(g + 1) * 512],
                     waits=bar1 + [prev])
        tb = kb.op("tensor", lambda e: e.matmul(
            psE[:, 0:512], ones_f[:, :], rows[0:1, 0:512], start=True, stop=True),
            waits=[ldq] + bar1)
        prev = kb.op("vector", lambda e, _g=g: e.tensor_copy(
            out=wq_t[:, _g * 512:(_g + 1) * 512], in_=psE[:, 0:512]),
            waits=[tb] + bar1)
    ldk = kb.dma("sync", rows[0:1, 0:512], kvnw_d[0:1, 0:512], waits=[prev])
    tb = kb.op("tensor", lambda e: e.matmul(
        psE[:, 0:512], ones_f[:, :], rows[0:1, 0:512], start=True, stop=True),
        waits=[ldk])
    t_wnorm = kb.op("vector", lambda e: e.tensor_copy(
        out=wkv_t[:, 0:512], in_=psE[:, 0:512]), waits=[tb])

    if upto == "p1bw":
        kb.dma("gpsimd", y_d[0:128, 0:D], wqa_s[:, 0:D], waits=kb.now_all(),
               sem="outd")
        kb.raw("sync", lambda e: e.wait_ge(kb.sems["outd"], 16))
        return _assemble(nc, kb, glob)

    last_res1 = {}
    last_sq1 = {}
    last_tr1 = {}
    last_te_tile = {}
    last_hread = {}
    last_ckvtr = []
    cqtr = []
    for t in range(NQT):
        xb = x1r[t % 2]
        own = True
        i_own = t
        lw = [last_res1.get(t - 2), last_sq1.get(t - 2), p1a_stores[t]]
        if t < 2:
            lw += bar1
        ld = kb.dma("sync", xb[:, :], x1s_d[t * 128:(t + 1) * 128, :], waits=lw)
        kb.op("vector", lambda e, _x=xb: e.reduce_sum(
            out=C(SUM), in_=_x[:, :], axis=AX), waits=[ld])
        ts_sq = kb.op("scalar", lambda e, _x=xb: e.activation(
            out=stage1[:, :], in_=_x[:, :], func=AF.Square, accum_out=C(SQ)),
            waits=[ld, last_res1.get(t - 1)])
        last_sq1[t] = ts_sq
        kb.op("vector", lambda e: e.tensor_scalar(
            out=C(MEAN), in0=C(SUM), scalar1=1.0 / D, scalar2=None, op0=MUL))
        kb.op("vector", lambda e: e.tensor_scalar(
            out=C(M2), in0=C(MEAN), scalar1=C(MEAN), scalar2=None, op0=MUL))
        kb.op("vector", lambda e: e.tensor_scalar(
            out=C(VAR), in0=C(SQ), scalar1=1.0 / D, scalar2=EPS_LN,
            op0=MUL, op1=ADD), waits=[ts_sq])
        t_var = kb.op("vector", lambda e: e.tensor_tensor(
            out=C(VAR), in0=C(VAR), in1=C(M2), op=SUB))
        t_sd = kb.op("scalar", lambda e: e.activation(
            out=C(SD), in_=C(VAR), func=AF.Sqrt), waits=[t_var])
        t_rstd = kb.op("vector", lambda e: e.reciprocal(out=C(RSTD), in_=C(SD)),
                       waits=[t_sd])
        kb.op("vector", lambda e, _x=xb: e.tensor_scalar(
            out=stage1[:, :], in0=_x[:, :], scalar1=C(MEAN), scalar2=None, op0=SUB),
            waits=[ts_sq])
        h_tm = h_tm2[t % 2]
        h_fm = h_fm2[t % 2]
        t_h = kb.op("vector", lambda e, _ht=h_tm: e.tensor_scalar(
            out=_ht[:, :], in0=stage1[:, :], scalar1=C(RSTD), scalar2=None, op0=MUL),
            waits=list(last_tr1.get(t - 2) or []))
        last_res1[t] = t_h
        t_htr = kb.dma("scalar",
                       h_fm[:, :].rearrange("p (t c) -> p t c", t=16),
                       h_tm[:, :].rearrange("p (t c) -> p t c", t=16),
                       waits=[t_h, last_hread.get(t - 2)], transpose=True)
        hfm_snap = [t_htr]
        last_tr1[t] = hfm_snap

        mw = list(hfm_snap) + wl_toks + ps_gate[0]
        tok_mm = None
        for ic in range(16):
            tok_mm = kb.op("tensor", lambda e, _ic=ic, _hf=h_fm: e.matmul(
                psA[:, :], _hf[:, _ic * 128:(_ic + 1) * 128],
                wkva_s[:, _ic * LKV:(_ic + 1) * LKV],
                start=(_ic == 0), stop=(_ic == 15)), waits=mw if ic == 0 else [])
        t_kvc = kb.op("vector", lambda e: e.tensor_copy(
            out=stage1[:, 0:512], in_=psA[:, :]), waits=[tok_mm])
        ps_gate[0] = [t_kvc]
        kb.op("vector", lambda e: e.tensor_tensor(
            out=stage1[:, 512:1024], in0=stage1[:, 0:512], in1=stage1[:, 0:512],
            op=MUL))
        t_sqkv = kb.op("vector", lambda e: e.reduce_sum(
            out=C(SQ2), in_=stage1[:, 512:1024], axis=AX))
        t_ms = kb.op("vector", lambda e: e.tensor_scalar(
            out=C(SD), in0=C(SQ2), scalar1=1.0 / LKV, scalar2=EPS_RMS,
            op0=MUL, op1=ADD), waits=[t_sqkv])
        t_sd2 = kb.op("scalar", lambda e: e.activation(
            out=C(SD), in_=C(SD), func=AF.Sqrt), waits=[t_ms])
        t_rr = kb.op("vector", lambda e: e.reciprocal(out=C(RC), in_=C(SD)),
                     waits=[t_sd2])
        t_n1 = kb.op("vector", lambda e: e.tensor_scalar(
            out=stage1[:, 0:512], in0=stage1[:, 0:512], scalar1=C(RC), scalar2=None,
            op0=MUL), waits=[t_rr])
        t_ckv = kb.op("vector", lambda e: e.tensor_tensor(
            out=ckvn_tm[:, :], in0=stage1[:, 0:512], in1=wkv_t[:, :], op=MUL),
            waits=[t_wnorm] + last_ckvtr)
        t_ckvtr = kb.dma(
            "scalar",
            ckv_own[:, :].rearrange("p (ci f) -> p ci f", ci=4)[:, :,
                                                               t * 128:(t + 1) * 128],
            ckvn_tm[:, :].rearrange("p (ci c) -> p ci c", ci=4),
            waits=[t_ckv] + last_ckvtr, transpose=True)
        last_ckvtr = [t_ckvtr]

        mw = list(hfm_snap) + wl_toks + ps_gate[1]
        tok_kr = None
        for ic in range(16):
            tok_kr = kb.op("tensor", lambda e, _ic=ic, _hf=h_fm: e.matmul(
                psB[0:64, 0:128], wkr_s[:, _ic * DR:(_ic + 1) * DR],
                _hf[:, _ic * 128:(_ic + 1) * 128],
                start=(_ic == 0), stop=(_ic == 15)), waits=mw if ic == 0 else [])
        ks = slice(t * 128, (t + 1) * 128)
        kb.op("vector", lambda e, _ks=ks: e.tensor_tensor(
            out=stage1[0:32, 0:128], in0=psB[0:32, 0:128], in1=cksk[0:32, _ks], op=MUL),
            waits=[tok_kr] + wl_toks)
        kb.op("vector", lambda e, _ks=ks: e.tensor_tensor(
            out=stage1[0:32, 128:256], in0=psB[32:64, 0:128], in1=cksk[32:64, _ks], op=MUL))
        kb.op("vector", lambda e, _ks=ks: e.tensor_tensor(
            out=krot_own[0:32, _ks], in0=stage1[0:32, 0:128],
            in1=stage1[0:32, 128:256], op=SUB))
        kb.op("vector", lambda e, _ks=ks: e.tensor_tensor(
            out=stage1[0:32, 0:128], in0=psB[32:64, 0:128], in1=cksk[0:32, _ks], op=MUL))
        t_r5 = kb.op("vector", lambda e, _ks=ks: e.tensor_tensor(
            out=stage1[0:32, 128:256], in0=psB[0:32, 0:128], in1=cksk[32:64, _ks], op=MUL))
        t_r6 = kb.op("vector", lambda e, _ks=ks: e.tensor_tensor(
            out=krot_own[32:64, _ks], in0=stage1[0:32, 0:128],
            in1=stage1[0:32, 128:256], op=ADD))
        ps_gate[1] = [t_r5]

        if own:
            skip_tr = upto.startswith("p1bq")
            cp_q = []
            for og in range(3):
                par = og % 2
                ps = ps_pool[par]
                mw = list(hfm_snap) + ps_gate[par]
                tok_q = None
                for ic in range(16):
                    tok_q = kb.op("tensor", lambda e, _ic=ic, _og=og, _ps=ps,
                                  _hf=h_fm: e.matmul(
                        _ps[:, :], _hf[:, _ic * 128:(_ic + 1) * 128],
                        wqa_s[:, _ic * LQ + _og * 512: _ic * LQ + (_og + 1) * 512],
                        start=(_ic == 0), stop=(_ic == 15)),
                        waits=mw if ic == 0 else [])
                t_cp = kb.op("vector", lambda e, _og=og, _ps=ps: e.tensor_copy(
                    out=cqs[:, _og * 512:(_og + 1) * 512], in_=_ps[:, :]),
                    waits=[tok_q])
                ps_gate[par] = [t_cp]
                cp_q.append(t_cp)
            kb.op("vector", lambda e: e.tensor_tensor(
                out=stage1[:, 512:512 + LQ], in0=cqs[:, :], in1=cqs[:, :], op=MUL),
                waits=[cp_q[2]])
            kb.op("vector", lambda e: e.reduce_sum(
                out=C(SQ2), in_=stage1[:, 512:512 + LQ], axis=AX))
            t_msq = kb.op("vector", lambda e: e.tensor_scalar(
                out=C(SD), in0=C(SQ2), scalar1=1.0 / LQ, scalar2=EPS_RMS,
                op0=MUL, op1=ADD))
            t_sdq = kb.op("scalar", lambda e: e.activation(
                out=C(SD), in_=C(SD), func=AF.Sqrt), waits=[t_msq])
            t_rq = kb.op("vector", lambda e: e.reciprocal(out=C(RC), in_=C(SD)),
                         waits=[t_sdq])
            t_cqn = None
            for og in range(3):
                kb.op("vector", lambda e, _og=og: e.tensor_scalar(
                    out=stage1[:, _og * 512:(_og + 1) * 512],
                    in0=cqs[:, _og * 512:(_og + 1) * 512],
                    scalar1=C(RC), scalar2=None, op0=MUL),
                    waits=[t_rq, cp_q[og]])
                t_cqn = kb.op("vector", lambda e, _og=og: e.tensor_tensor(
                    out=cqn_tm[:, _og * 512:(_og + 1) * 512],
                    in0=stage1[:, _og * 512:(_og + 1) * 512],
                    in1=wq_t[:, _og * 512:(_og + 1) * 512], op=MUL),
                    waits=[t_wnorm] + (cqtr if og == 0 else []))
            if not skip_tr:
                cqtr = [kb.dma(
                    "scalar",
                    cq_fm[:, :].rearrange("p (ci f) -> p ci f", ci=12)[
                        :, :, i_own * 128:(i_own + 1) * 128],
                    cqn_tm[:, :].rearrange("p (ci c) -> p ci c", ci=12),
                    waits=[t_cqn], transpose=True)]
        last_res1[t] = kb.now("ve")[0]
        last_te_tile[t] = kb.now("te")[0]
        last_hread[t] = kb.now("te")[0]
        if upto.startswith("p1b") and upto[3:].isdigit() and t == int(upto[3:]) - 1:
            break
        if upto.startswith("p1bq") and t == 3:
            break

    exw = kb.now("ve", "se", "t0", "t1", "t2", "t3")
    st_ckv = kb.dma("scalar", cex_i, ckv_own[:, :], waits=exw)
    st_kr = kb.dma("scalar", krx_i, krot_own[:, :], waits=exw)
    rg = [[0, 1, 2, 3], [4, 5, 6, 7]]
    kb.raw("gpsimd", lambda e, _a=st_ckv, _b=st_kr: e.wait_ge(
        kb.sems[_a[0]], _a[1]))
    kb.raw("gpsimd", lambda e, _b=st_kr: e.wait_ge(kb.sems[_b[0]], _b[1]))
    kb.cnt["gc"] += 2

    def _coll(e, _rg=rg):
        e.collective_compute("AllGather", AluOpType.bypass, replica_groups=_rg,
                             ins=[cex_i], outs=[cex_o]).then_inc(kb.sems["gc"], 1)
        e.collective_compute("AllGather", AluOpType.bypass, replica_groups=_rg,
                             ins=[krx_i], outs=[krx_o]).then_inc(kb.sems["gc"], 1)
    kb.raw("gpsimd", _coll)
    coll_tok = ("gc", kb.cnt["gc"])
    gl = []
    for b in range(4):
        for ci in range(4):
            gl.append(kb.dma(
                "sync",
                ckv_fm[:, ci * T:(ci + 1) * T].rearrange(
                    "p (t f) -> p t f", f=512)[:, :, b * 128:(b + 1) * 128],
                cex_o[b * 128:(b + 1) * 128, ci * 1024:(ci + 1) * 1024].rearrange(
                    "p (t c) -> p t c", t=8),
                waits=[coll_tok]))
        gl.append(kb.dma(
            "sync",
            krot_fm[0:64, :].rearrange("p (t f) -> p t f", f=512)[
                :, :, b * 128:(b + 1) * 128],
            krx_o[b * 64:(b + 1) * 64, :].rearrange("p (t c) -> p t c", t=8),
            waits=[coll_tok]))
    gl_toks = kb.now("s0", "s1", "s2", "s3")
    kb.op("vector", lambda e: e.memset(krot_fm[64:128, :], 0.0))

    bar2 = kb.now_all()
    if upto.startswith("p1b") and upto != "p1bw":
        kb.dma("gpsimd", y_d[0:128, 0:D], ckv_fm[:, 0:D],
               waits=bar2, sem="outd")
        kb.raw("sync", lambda e: e.wait_ge(kb.sems["outd"], 16))
        return _assemble(nc, kb, glob)
    p1b.close()

    p2 = contextlib.ExitStack()
    wqb_s = [p2.enter_context(nc.sbuf_tensor(f"wqb{i}", [128, 12 * DQK], BF16))
             for i in range(2)]
    wkvb_s = [p2.enter_context(nc.sbuf_tensor(f"wkvb{i}", [128, 4 * 256], BF16))
              for i in range(2)]
    qfa = p2.enter_context(nc.sbuf_tensor("qfa", [128, 1024], BF16))
    qfb = p2.enter_context(nc.sbuf_tensor("qfb", [128, 1024], BF16))
    kp_fm = p2.enter_context(nc.sbuf_tensor("kp_fm", [128, T], BF16))
    v_tm = p2.enter_context(nc.sbuf_tensor("v_tm", [128, 32 * 128], BF16))
    S_2 = [p2.enter_context(nc.sbuf_tensor(f"S_{i}", [128, T], F32))
           for i in range(2)]
    P_s = [p2.enter_context(nc.sbuf_tensor(f"P{j}", [128, T], BF16)) for j in range(4)]
    PT_n = p2.enter_context(nc.sbuf_tensor("PT_n", [128, 4 * 2048], BF16))
    v_fm = p2.enter_context(nc.sbuf_tensor("v_fm", [128, 16 * 128], BF16))
    o_hd = [p2.enter_context(nc.sbuf_tensor(f"o_hd{i}", [128, 1024], BF16))
            for i in range(2)]
    rsumb = p2.enter_context(nc.sbuf_tensor("rsumb", [128, 512], F32))
    rs4 = p2.enter_context(nc.sbuf_tensor("rs4", [128, 8], F32))
    mask_b = p2.enter_context(nc.sbuf_tensor("mask_b", [128, 512], BF16))
    id_f = p2.enter_context(nc.sbuf_tensor("id_f", [128, 128], F32))
    cqsq = p2.enter_context(nc.sbuf_tensor("cqsq", [64, 1024], BF16))
    cqsq2 = p2.enter_context(nc.sbuf_tensor("cqsq2", [64, 1024], BF16))
    rope_t = p2.enter_context(nc.sbuf_tensor("rope_t", [64, 1024], F32))
    qfb_f = p2.enter_context(nc.sbuf_tensor("qfb_f", [64, 1024], F32))

    t_mk = kb.dma("gpsimd", mask_b[:, :], mask_d, waits=bar2)
    t_idf = kb.dma("sync", id_f[:, :], eye_d, waits=bar2)
    kb.op("vector", lambda e: e.memset(qfb[64:128, :], 0.0))
    t_cqs = kb.dma("gpsimd", cqsq[0:32, :], cqt_d)
    kb.dma("gpsimd", cqsq[32:64, :], sqt_d)
    kb.dma("gpsimd", cqsq2[32:64, :], cqt_d)
    t_qt = kb.dma("gpsimd", cqsq2[0:32, :], sqt_d)

    def load_head_w(h, waits):
        for ic in range(12):
            kb.dma("gpsimd", wqb_s[h % 2][:, ic * DQK:(ic + 1) * DQK],
                   wqb_d[ic * 128:(ic + 1) * 128, h * DQK:(h + 1) * DQK],
                   waits=waits if ic == 0 else [])
        for ci in range(4):
            kb.dma("gpsimd", wkvb_s[h % 2][:, ci * 256:(ci + 1) * 256],
                   wkvb_d[ci * 128:(ci + 1) * 128, h * 256:(h + 1) * 256])
        return kb.now("g0", "g1", "g2", "g3")

    head_wload = {0: load_head_w(0, bar2 + [t_qt])}
    head_te_end = {}
    exp_gate = {0: None, 1: None}
    oev_gate = {0: None, 1: None}
    pslot_gate = {j: None for j in range(4)}
    pt_gate = None
    s_gate = {0: [], 1: [], 2: [], 3: []}
    pv_prev_half = None
    t_ptz = kb.op("vector", lambda e: e.memset(PT_n[:, :], 0.0), waits=bar2)
    pt_zero = {j: t_ptz for j in range(4)}
    ost_gate = {(p, g): None for p in range(2) for g in range(2)}
    v_gate = []
    prev_rw4 = None
    prev_tr4 = None
    prev_rsb = None
    prev_oev = None

    for h in range(H):
        wq = wqb_s[h % 2]
        wkv = wkvb_s[h % 2]
        wld = head_wload[h]
        if h + 1 < H:
            head_wload[h + 1] = load_head_w(h + 1, [head_te_end.get(h - 1)])
        te_prev = head_te_end.get(h - 1)
        t_qfa = {}
        for qs in range(2):
            par = qs % 2
            ps = ps_pool[par]
            mw = wld + ps_gate[par]
            tok = None
            for ic in range(12):
                tok = kb.op("tensor", lambda e, _ic=ic, _qs=qs, _ps=ps, _wq=wq: e.matmul(
                    _ps[:, :], _wq[:, _ic * DQK:_ic * DQK + 128],
                    cq_fm[:, _ic * 1024 + _qs * 512: _ic * 1024 + (_qs + 1) * 512],
                    start=(_ic == 0), stop=(_ic == 11)),
                    waits=mw if ic == 0 else [])
            t_qfa[qs] = kb.op("vector", lambda e, _qs=qs, _ps=ps: e.tensor_copy(
                out=qfa[:, _qs * 512:(_qs + 1) * 512], in_=_ps[:, :]),
                waits=[tok, te_prev])
            ps_gate[par] = [t_qfa[qs]]
        t_qf_last = None
        for qs in range(2):
            par = qs % 2
            ps = ps_pool[par]
            mw = ps_gate[par]
            tok = None
            for ic in range(12):
                tok = kb.op("tensor", lambda e, _ic=ic, _qs=qs, _ps=ps, _wq=wq: e.matmul(
                    _ps[0:64, :], _wq[:, _ic * DQK + 128:(_ic + 1) * DQK],
                    cq_fm[:, _ic * 1024 + _qs * 512: _ic * 1024 + (_qs + 1) * 512],
                    start=(_ic == 0), stop=(_ic == 11)),
                    waits=mw if ic == 0 else [])
            sl = slice(qs * 512, (qs + 1) * 512)
            t_qe = kb.op("vector", lambda e, _ps=ps, _qs=qs: e.tensor_copy(
                out=qfb_f[:, _qs * 512:(_qs + 1) * 512], in_=_ps[0:64, :]),
                waits=[tok])
            ps_gate[par] = [t_qe]
        kb.op("vector", lambda e: e.tensor_tensor(
            out=qfb[0:32, :], in0=qfb_f[0:32, :], in1=cqsq[0:32, :], op=MUL),
            waits=[t_qt, t_cqs])
        kb.op("vector", lambda e: e.tensor_tensor(
            out=rope_t[0:32, :], in0=qfb_f[32:64, :], in1=cqsq[32:64, :], op=MUL))
        kb.op("vector", lambda e: e.tensor_tensor(
            out=qfb[0:32, :], in0=qfb[0:32, :], in1=rope_t[0:32, :], op=SUB))
        kb.op("vector", lambda e: e.tensor_tensor(
            out=qfb[32:64, :], in0=qfb_f[32:64, :], in1=cqsq2[32:64, :], op=MUL))
        kb.op("vector", lambda e: e.tensor_tensor(
            out=rope_t[32:64, :], in0=qfb_f[0:32, :], in1=cqsq2[0:32, :], op=MUL))
        t_qf_last = kb.op("vector", lambda e: e.tensor_tensor(
            out=qfb[32:64, :], in0=qfb[32:64, :], in1=rope_t[32:64, :], op=ADD))
        t_kp = {}
        for kc8 in range(8):
            par = kc8 % 2
            ps = ps_pool[par]
            mw = wld + ps_gate[par]
            tok = None
            for ci in range(4):
                tok = kb.op("tensor", lambda e, _ci=ci, _k=kc8, _ps=ps, _w=wkv: e.matmul(
                    _ps[:, :], _w[:, _ci * 256:_ci * 256 + 128],
                    ckv_fm[:, _ci * T + _k * 512: _ci * T + (_k + 1) * 512],
                    start=(_ci == 0), stop=(_ci == 3)),
                    waits=mw if ci == 0 else [])
            t_kp[kc8] = kb.op("vector", lambda e, _k=kc8, _ps=ps: e.tensor_copy(
                out=kp_fm[:, _k * 512:(_k + 1) * 512], in_=_ps[:, :]), waits=[tok],
                relax=(kc8 > 0))
            ps_gate[par] = [t_kp[kc8]]
        t_vfm = {}
        vtr = []
        for kc8 in range(8):
            par = kc8 % 2
            ps = ps_pool[par]
            mw = list(ps_gate[par])
            if kc8 == 4:
                mw.append(vtr[0])
            tok = None
            for ci in range(4):
                tok = kb.op("tensor", lambda e, _ci=ci, _k=kc8, _ps=ps, _w=wkv: e.matmul(
                    _ps[:, :], _w[:, _ci * 256 + 128:(_ci + 1) * 256],
                    ckv_fm[:, _ci * T + _k * 512: _ci * T + (_k + 1) * 512],
                    start=(_ci == 0), stop=(_ci == 3)),
                    waits=mw if ci == 0 else [])
            t_vfm[kc8] = kb.op("vector", lambda e, _k=kc8, _ps=ps: e.tensor_copy(
                out=v_fm[:, (_k % 4) * 512:(_k % 4 + 1) * 512], in_=_ps[:, :]),
                waits=[tok] + (v_gate if kc8 == 0 else [])
                + ([vtr[0]] if kc8 == 4 else []), relax=(kc8 % 4 != 0))
            ps_gate[par] = [t_vfm[kc8]]
            if kc8 == 3 or kc8 == 7:
                half = kc8 // 4
                vtr.append(kb.dma(
                    "scalar",
                    v_tm[:, half * 2048:(half + 1) * 2048].rearrange(
                        "p (t c) -> p t c", t=16),
                    v_fm[:, :].rearrange("p (t c) -> p t c", t=16),
                    waits=[t_vfm[kc8]] + ([te_prev] if half == 0 else []),
                    transpose=True))
        v_ready = [vtr[0], vtr[1]]
        v_gate = [vtr[1]]

        grp = {g: dict(msk=[None] * 4, ea=[None] * 4, pad=[None] * 4,
                       trw=[None] * 4, t_rs=None, nkt=2048 * (g + 1) // 128)
               for g in range(2)}
        tile_st = {}
        BIAS_COLS = [BIAS, ACC]

        def emit_A(g, j):
            s_gate[0] = list(set_tokens(s_gate[0], ps_gate[0]))
            s_gate[1] = list(set_tokens(s_gate[1], ps_gate[1]))
            i = 4 * g + j
            K = 512 * (i + 1)
            qsl = slice(i * 128, (i + 1) * 128)
            Sb = S_2[i % 2]
            sc_last = None
            npair = 0
            for kc8 in range(i + 1):
                slot = kc8 % 4
                psx = psS1 if slot < 2 else psS2
                reg = psx[:, (slot % 2) * 512:(slot % 2 + 1) * 512]
                mw = [t_kp[kc8], t_qf_last] + s_gate[slot]
                kb.op("tensor", lambda e, _k=kc8, _r=reg, _q=qsl: e.matmul(
                    _r, qfa[:, _q], kp_fm[:, _k * 512:(_k + 1) * 512],
                    start=True, stop=False), waits=mw)
                tok = kb.op("tensor", lambda e, _k=kc8, _r=reg, _q=qsl: e.matmul(
                    _r, qfb[:, _q], krot_fm[:, _k * 512:(_k + 1) * 512],
                    start=False, stop=True))
                if slot % 2 == 1 or kc8 == i:
                    w = 1024 if slot % 2 == 1 else 512
                    sc_last = kb.op(
                        "vector", lambda e, _k=kc8, _S=Sb, _w=w, _n=npair,
                        _px=psx: e.tensor_scalar(
                            out=_S[:, (_k + 1) * 512 - _w:(_k + 1) * 512],
                            in0=_px[:, 0:_w],
                            scalar1=1.0, scalar2=NEG_BIG, op0=MUL, op1=MAXOP,
                            accum_out=m_cols[:, _n:_n + 1]),
                        waits=[tok, exp_gate[i % 2]], relax=(npair > 0))
                    s_gate[slot - (slot % 2)] = [sc_last]
                    s_gate[slot - (slot % 2) + 1] = [sc_last]
                    npair += 1
            ps_gate[0] = s_gate[0]
            ps_gate[1] = s_gate[1]
            bc = BIAS_COLS[i % 2]
            t_bias = kb.op("vector", lambda e, _b=bc, _n=npair: e.tensor_scalar(
                out=m_cols[:, 0:_n], in0=m_cols[:, 0:_n], scalar1=-SCALE,
                scalar2=3.0e38, op0=MUL, op1=AluOpType.min,
                accum_out=C(_b)), waits=[sc_last, exp_gate[i % 2]])
            tile_st[(g, j)] = dict(K=K, Sb=Sb, bias=t_bias, bc=bc)

        prev_sum = [None]

        def emit_B(g, j):
            i = 4 * g + j
            st = tile_st[(g, j)]
            K, Sb, bc = st["K"], st["Sb"], st["bc"]
            maxK = 2048 * (g + 1)
            Pj = P_s[j]
            pw = [st["bias"], prev_sum[0]] + (pslot_gate[j] or [])
            t_ea = None
            if i > 0:
                t_ea = kb.op("scalar", lambda e, _P=Pj, _K=K, _S=Sb,
                             _b=bc: e.activation(
                    out=_P[:, 0:_K - 512], in_=_S[:, 0:_K - 512], func=AF.Exp,
                    bias=C(_b), scale=SCALE, accum_out=C(SA)), waits=pw)
            t_ed = kb.op("scalar", lambda e, _P=Pj, _K=K, _S=Sb, _b=bc: e.activation(
                out=_P[:, _K - 512:_K], in_=_S[:, _K - 512:_K], func=AF.Exp,
                bias=C(_b), scale=SCALE), waits=pw)
            exp_gate[i % 2] = t_ed
            t_msk = kb.op("vector", lambda e, _P=Pj, _K=K: e.scalar_tensor_tensor(
                out=_P[:, _K - 512:_K], in0=_P[:, _K - 512:_K], scalar=1.0,
                in1=mask_b[:, :], op0=MUL, op1=MUL, accum_out=C(SK)),
                waits=[t_ed, t_mk])
            grp[g]["msk"][j] = t_msk
            grp[g]["ea"][j] = t_ea
            if K < maxK:
                grp[g]["pad"][j] = kb.op("vector", lambda e, _P=Pj, _K=K, _M=maxK:
                                         e.memset(_P[:, _K:_M], 0.0),
                                         waits=pslot_gate[j] or [])
            if i > 0:
                t_sum = kb.op("vector", lambda e: e.tensor_tensor(
                    out=C(SSUM), in0=C(SK), in1=C(SA), op=ADD), waits=[t_ea])
            else:
                t_sum = kb.op("vector", lambda e: e.tensor_copy(
                    out=C(SSUM), in_=C(SK)))
            prev_sum[0] = t_sum
            grp[g]["t_rs"] = kb.op("vector", lambda e, _c=i: e.reciprocal(
                out=rs4[:, _c:_c + 1], in_=C(SSUM)),
                waits=[prev_tr4] if i == 0 else [])
            if g == 0:
                used = (j + 1) * 4
                grp[0]["trw"][j] = kb.dma(
                    "scalar",
                    PT_n[:, j * 2048: j * 2048 + used * 128].rearrange(
                        "p (t c) -> p t c", t=used),
                    P_s[j][:, 0:used * 128].rearrange("p (t c) -> p t c", t=used),
                    waits=[t_msk, t_ea, grp[0]["pad"][j], pt_gate,
                           pt_zero[j]], transpose=True)
                pslot_gate[j] = [grp[0]["trw"][j]]

        tiles = [(g, j) for g in range(2) for j in range(4)]
        emit_A(0, 0)
        for idx in range(1, 8):
            emit_A(*tiles[idx])
            emit_B(*tiles[idx - 1])
        emit_B(*tiles[7])

        for g in range(2):
            st = grp[g]
            nkt = st["nkt"]
            po = ps_o[g]
            pv_tok = {}
            for half in range(nkt // 16):
                if g == 1:
                    trw = []
                    if half == 1:
                        trw_h0 = list(pslot_gate[0] + pslot_gate[1]
                                      + pslot_gate[2] + pslot_gate[3])
                        for j in range(3):
                            pt_zero[j] = kb.op(
                                "vector", lambda e, _j=j: e.memset(
                                    PT_n[:, _j * 2048 + (_j + 1) * 512:
                                         (_j + 1) * 2048], 0.0),
                                waits=([pt_gate] + trw_h0)
                                if _j_first(j) else [])
                    for j in range(4):
                        used = 16 if half == 0 else (j + 1) * 4
                        tw = [st["msk"][j], st["ea"][j], st["pad"][j], pt_gate]
                        if half == 1:
                            tw.append(pt_zero[j])
                        trw.append(kb.dma(
                            "scalar",
                            PT_n[:, j * 2048: j * 2048 + used * 128].rearrange(
                                "p (t c) -> p t c", t=used),
                            P_s[j][:, half * 2048:
                                   half * 2048 + used * 128].rearrange(
                                "p (t c) -> p t c", t=used),
                            waits=tw, transpose=True))
                        pslot_gate[j] = [trw[-1]]
                else:
                    trw = st["trw"]
                for kk in range(16):
                    kt = half * 16 + kk
                    rhs = PT_n[:, :].rearrange("p (j t c) -> p j t c",
                                               j=4, t=16)[:, :, kk, :]
                    pvw = list(trw) if kk == 0 else []
                    if kk == 0:
                        pvw += [pt_zero[j2] for j2 in range(4)]
                    if kt == 0:
                        pvw += [oev_gate[g]] + list(v_ready)
                    pv_tok[kt] = kb.op("tensor", lambda e, _kt=kt, _po=po, _r=rhs,
                                       _n=nkt: e.matmul(
                        _po[:, 0:512], v_tm[:, _kt * 128:(_kt + 1) * 128], _r,
                        start=(_kt == 0), stop=(_kt == _n - 1)),
                        waits=pvw)
                pt_gate = pv_tok[half * 16 + 15]
                pv_prev_half = pt_gate
            t_tr4 = None
            for j in range(4):
                t_tr4 = kb.op("tensor", lambda e, _j=j, _c=g * 4 + j: e.transpose(
                    psF[0:1, _j * 128:(_j + 1) * 128], rs4[:, _c:_c + 1],
                    id_f[:, :]),
                    waits=[st["t_rs"], t_idf] + ([prev_rw4] if j == 0 else []))
            prev_tr4 = t_tr4
            t_rw4 = kb.op("vector", lambda e: e.tensor_copy(
                out=rows[0:1, 0:512], in_=psF[0:1, 0:512]), waits=[t_tr4])
            prev_rw4 = t_rw4
            t_bc = None
            for j in range(4):
                t_bc = kb.op("tensor", lambda e, _j=j: e.matmul(
                    psE[:, _j * 128:(_j + 1) * 128], ones_f[:, :],
                    rows[0:1, _j * 128:(_j + 1) * 128], start=True, stop=True),
                    waits=[t_rw4] + ([prev_rsb] if j == 0 else []))
            t_rsb = kb.op("vector", lambda e: e.tensor_copy(
                out=rsumb[:, :], in_=psE[:, 0:512]), waits=[t_bc, prev_oev])
            prev_rsb = t_rsb
            t_oev = kb.op("vector", lambda e, _g=g, _po=po, _ob=o_hd[h % 2]:
                          e.tensor_tensor(
                out=_ob[:, _g * 512:(_g + 1) * 512],
                in0=_po[:, 0:512], in1=rsumb[:, :], op=MUL),
                waits=[t_rsb, pv_tok[nkt - 1], ost_gate[(h % 2, g)]])
            ost_gate[(h % 2, g)] = kb.dma(
                "sync", o_d[:, h * 1024 + g * 512: h * 1024 + (g + 1) * 512],
                o_hd[h % 2][:, g * 512:(g + 1) * 512], waits=[t_oev])
            oev_gate[g] = t_oev
            prev_oev = t_oev
        head_te_end[h] = kb.now("te")[0]

    bar3 = kb.now_all()
    if upto == "p2":
        kb.dma("gpsimd", y_d[0:128, 0:D], v_tm[:, 0:D],
               waits=bar3, sem="outd")
        kb.raw("sync", lambda e: e.wait_ge(kb.sems["outd"], 16))
        return _assemble(nc, kb, glob)
    p2.close()

    p3a = contextlib.ExitStack()
    wo_s = p3a.enter_context(nc.sbuf_tensor("wo_s", [128, 16 * D], BF16))
    x1o = [p3a.enter_context(nc.sbuf_tensor(f"x1o{i}", [128, D], F32))
           for i in range(2)]
    o_t = [p3a.enter_context(nc.sbuf_tensor(f"o_t{i}", [128, 16 * 128], BF16))
           for i in range(2)]
    x2t = [p3a.enter_context(nc.sbuf_tensor(f"x2t{i}", [128, D], F32))
           for i in range(2)]
    for ic in range(16):
        kb.dma("gpsimd", wo_s[:, ic * D:(ic + 1) * D],
               wo_d[ic * 128:(ic + 1) * 128, :],
               waits=bar3 if ic == 0 else [])
    wol_toks = kb.now("g0", "g1", "g2", "g3")
    last_r3 = {}
    last_st3 = {}
    last_omm = {}
    p3a_stores = []
    wgate = {0: list(ps_gate[0]) + list(ps_gate[1]),
             2: list(ps_gate[0]) + list(ps_gate[1])}
    for i in range(NQT):
        xb = x1o[i % 2]
        ob = o_t[i % 2]
        x2 = x2t[i % 2]
        lw = [last_r3.get(i - 2), p1a_stores[i]]
        if i < 2:
            lw += bar3
        ld = kb.dma("sync", xb[:, :], x1s_d[i * 128:(i + 1) * 128, :], waits=lw)
        ldo = kb.dma("sync",
                     ob[:, :].rearrange("p (hc c) -> p hc c", hc=16),
                     o_d[:, :].rearrange("p (hc f) -> p hc f", hc=16)[
                         :, :, i * 128:(i + 1) * 128],
                     waits=[last_omm.get(i - 2)] + (bar3 if i < 2 else []))
        for og in range(4):
            psx = psS1 if og < 2 else psS2
            reg = psx[:, (og % 2) * 512:(og % 2 + 1) * 512]
            mw = [ld, ldo] + wol_toks + wgate[og - (og % 2)]
            tok = None
            for hc in range(16):
                tok = kb.op("tensor", lambda e, _hc=hc, _og=og, _r=reg,
                            _ob=ob: e.matmul(
                    _r, _ob[:, _hc * 128:(_hc + 1) * 128],
                    wo_s[:, _hc * D + _og * 512: _hc * D + (_og + 1) * 512],
                    start=(_hc == 0), stop=(_hc == 15)),
                    waits=mw if hc == 0 else [])
            if og % 2 == 1:
                base = (og - 1) * 512
                ew = [tok]
                if og == 1:
                    ew.append(last_st3.get(i - 2))
                t_e = kb.op("vector", lambda e, _b=base, _px=psx, _x=xb,
                            _x2=x2: e.tensor_tensor(
                    out=_x2[:, _b:_b + 1024], in0=_px[:, 0:1024],
                    in1=_x[:, _b:_b + 1024], op=ADD), waits=ew)
                wgate[og - 1] = [t_e]
                last_r3[i] = t_e
        last_omm[i] = tok
        last_st3[i] = kb.dma("scalar", x2s_d[i * 128:(i + 1) * 128, :], x2[:, :],
                             waits=[last_r3[i]])
        p3a_stores.append(last_st3[i])
    ps_gate[0] = wgate[0] + wgate[2]
    ps_gate[1] = wgate[0] + wgate[2]
    bar4 = kb.now_all()
    if upto == "p3a":
        kb.dma("sync", y_d[0:128, :], x2s_d[0:128, :], waits=bar4, sem="outd")
        kb.raw("sync", lambda e: e.wait_ge(kb.sems["outd"], 16))
        return _assemble(nc, kb, glob)
    p3a.close()
    res.close()

    p3b = contextlib.ExitStack()
    w_tT2 = p3b.enter_context(nc.sbuf_tensor("w_tT2", [128, 16 * D], BF16))
    for ic in range(16):
        kb.dma("sync", w_tT2[:, ic * D:(ic + 1) * D],
               wts_d[ic * 128:(ic + 1) * 128, :],
               waits=bar4 if ic == 0 else [])
    wrl_toks = kb.now("s0", "s1", "s2", "s3")
    ffn_phase(p3b, "fb", lambda t: x2s_d[t * 128:(t + 1) * 128, :],
              lambda t: p3a_stores[t],
              lambda t: y_d[t * 128:(t + 1) * 128, :], NQT, "outd",
              w_tT2, wrl_toks + bar4, load_ready=bar4)
    p3b.close()

    n_out = kb.cnt["outd"]
    kb.raw("sync", lambda e, _n=n_out: e.wait_ge(kb.sems["outd"], _n))
    return _assemble(nc, kb, glob)


def _assemble(nc, kb, glob):
    with contextlib.ExitStack() as semstk:
        for s in KB.ALL_SEMS:
            kb.sems[s] = semstk.enter_context(nc.semaphore(s))
        with nc.Block() as block:
            @block.tensor
            def _(eng):
                for f in kb.ops["tensor"]:
                    f(eng)

            @block.vector
            def _(eng):
                for f in kb.ops["vector"]:
                    f(eng)

            @block.scalar
            def _(eng):
                for f in kb.ops["scalar"]:
                    f(eng)

            @block.sync
            def _(eng):
                for f in kb.ops["sync"]:
                    f(eng)

            @block.gpsimd
            def _(eng):
                for f in kb.ops["gpsimd"]:
                    f(eng)
    return nc, kb


def make_in_maps(d):
    return _make_in_maps(**d)


def kernel(L0_attn_hidden_states, L0_attn_Wq_a, L0_attn_q_norm_w, L0_attn_Wq_b,
           L0_attn_Wk_rope, L0_attn_Wkv_a, L0_attn_kv_norm_w, L0_attn_Wkv_b,
           L0_attn_Wo, L0_ffn_weight, L0_ffn_bias):
    in_maps = _make_in_maps(L0_attn_hidden_states, L0_attn_Wq_a, L0_attn_q_norm_w,
                            L0_attn_Wq_b, L0_attn_Wk_rope, L0_attn_Wkv_a,
                            L0_attn_kv_norm_w, L0_attn_Wkv_b, L0_attn_Wo,
                            L0_ffn_weight, L0_ffn_bias)
    nc, _ = build_program()
    res = run_bass_kernel_spmd(nc, in_maps, core_ids=list(range(8)))

    out = np.empty((2, T, D), dtype=np.float32)
    for c in range(8):
        b, r = c // 4, c % 4
        yc = res.results[c]["y"]
        for i in range(NQT):
            g = 4 * i + r
            out[b, g * 128:(g + 1) * 128, :] = yc[i * 128:(i + 1) * 128, :]
    return out


def _make_in_maps(L0_attn_hidden_states, L0_attn_Wq_a, L0_attn_q_norm_w, L0_attn_Wq_b,
                  L0_attn_Wk_rope, L0_attn_Wkv_a, L0_attn_kv_norm_w, L0_attn_Wkv_b,
                  L0_attn_Wo, L0_ffn_weight, L0_ffn_bias):
    x = np.ascontiguousarray(L0_attn_hidden_states, dtype=np.float32)
    weights = dict(
        wqa=np.ascontiguousarray(L0_attn_Wq_a, dtype=np.float32),
        qnw=np.ascontiguousarray(L0_attn_q_norm_w, dtype=np.float32).reshape(1, LQ),
        wqb=np.ascontiguousarray(L0_attn_Wq_b, dtype=np.float32),
        wkr=np.ascontiguousarray(L0_attn_Wk_rope, dtype=np.float32),
        wkva=np.ascontiguousarray(L0_attn_Wkv_a, dtype=np.float32),
        kvnw=np.ascontiguousarray(L0_attn_kv_norm_w, dtype=np.float32).reshape(1, LKV),
        wkvb=np.ascontiguousarray(L0_attn_Wkv_b, dtype=np.float32),
        wo=np.ascontiguousarray(L0_attn_Wo, dtype=np.float32),
        fw=np.ascontiguousarray(L0_ffn_weight, dtype=np.float32),
        fb=np.ascontiguousarray(L0_ffn_bias, dtype=np.float32).reshape(1, D),
        eye=np.eye(128, dtype=np.float32),
    )
    inv_freq = 1.0 / (10000.0 ** (np.arange(0, DR, 2, dtype=np.float64) / DR))

    in_maps = []
    for c in range(8):
        b, r = c // 4, c % 4
        own_idx = ((4 * np.arange(NQT)[:, None] + r) * 128
                   + np.arange(128)[None, :]).reshape(-1)
        x0c = np.ascontiguousarray(x[b].reshape(T, D)[own_idx])
        own_rows = own_idx.astype(np.float64)
        cqt = np.cos(inv_freq[:, None] * own_rows[None, :]).astype(np.float32)
        sqt = np.sin(inv_freq[:, None] * own_rows[None, :]).astype(np.float32)
        mask = np.zeros((128, 512), dtype=np.float32)
        for p in range(4):
            if p < r:
                mask[:, p * 128:(p + 1) * 128] = 1.0
            elif p == r:
                mask[:, p * 128:(p + 1) * 128] = np.tril(
                    np.ones((128, 128), dtype=np.float32))
        m = dict(weights)
        m.update(x0=x0c, maskc=mask, cqt=cqt, sqt=sqt)
        in_maps.append(m)
    return in_maps


if __name__ == "__main__":
    nc, kb = build_program()
    print("instruction counts:",
          {k: len(v) for k, v in kb.ops.items()},
          "sem counts:", kb.cnt)


# version 28
# speedup vs baseline: 1.0091x; 1.0091x over previous
import math
import os
import contextlib
import numpy as np
import concourse.bass as bass
import concourse.mybir as mybir
from concourse.bass_utils import run_bass_kernel_spmd
from concourse.alu_op_type import AluOpType

F32 = mybir.dt.float32
BF16 = mybir.dt.bfloat16
AF = mybir.ActivationFunctionType
AX = mybir.AxisListType.X
MUL = AluOpType.mult
ADD = AluOpType.add
SUB = AluOpType.subtract
GT = AluOpType.is_gt
LT = AluOpType.is_lt
MAXOP = AluOpType.max

D = 2048
T = 4096
H = 16
DR = 64
DQK = 192
LQ = 1536
LKV = 512
NT = 32
NQT = 8
SCALE = 1.0 / math.sqrt(DQK)
EPS_LN = 1e-5
EPS_RMS = 1e-6
NEG_BIG = -3.0e38

(SUM, SQ, SQ2, SQ3, MEAN, M2, VAR, SD, RSTD, RC, TCOL, CCOL, NTCOL,
 MX, BIAS, SA, SK, SSUM, RS, ACC, TMP) = range(21)


class KB:
    ENG_SEM = {"tensor": "te", "vector": "ve", "scalar": "se"}
    Q_SEMS = {"sync": ["s0", "s1", "s2", "s3"], "scalar": ["t0", "t1", "t2", "t3"],
              "gpsimd": ["g0", "g1", "g2", "g3"]}
    ALL_SEMS = ("te", "ve", "se", "gc", "outd", "s0", "s1", "s2", "s3",
                "t0", "t1", "t2", "t3", "g0", "g1", "g2", "g3")

    def __init__(self):
        self.ops = {k: [] for k in ("tensor", "vector", "scalar", "sync", "gpsimd")}
        self.cnt = {k: 0 for k in self.ALL_SEMS}
        self.rr = {k: 0 for k in self.Q_SEMS}
        self.lastw = {k: {} for k in self.ops}
        self.sems = {}

    def _wait(self, engine, waits):
        for w in waits:
            if w is None:
                continue
            s, v = w
            if v <= 0 or self.lastw[engine].get(s, 0) >= v:
                continue
            self.lastw[engine][s] = v

            def f(eng, _s=s, _v=v):
                eng.wait_ge(self.sems[_s], _v)
            self.ops[engine].append(f)

    def op(self, engine, fn, waits=(), relax=False):
        s = self.ENG_SEM[engine]
        if engine in ("vector", "scalar") and not relax:
            self._wait(engine, list(waits) + [(s, self.cnt[s])])
        else:
            self._wait(engine, waits)
        self.cnt[s] += 1
        tok = self.cnt[s]

        def f(eng, _fn=fn, _s=s):
            _fn(eng).then_inc(self.sems[_s], 1)
        self.ops[engine].append(f)
        return (s, tok)

    def dma(self, queue, out, in_, waits=(), sem=None, transpose=False):
        if sem is None:
            pool = self.Q_SEMS[queue]
            s = pool[self.rr[queue] % len(pool)]
            self.rr[queue] += 1
        else:
            s = sem
        prev = self.cnt[s]
        self._wait(queue, list(waits) + [(s, prev)])
        self.cnt[s] = prev + 16
        tok = self.cnt[s]

        def f(eng, _o=out, _i=in_, _s=s, _t=transpose):
            eng.dma_start(out=_o, in_=_i, transpose=_t).then_inc(self.sems[_s], 16)
        self.ops[queue].append(f)
        return (s, tok)

    def raw(self, engine, fn, waits=()):
        self._wait(engine, waits)
        self.ops[engine].append(fn)

    def now(self, *sems):
        return [(s, self.cnt[s]) for s in sems]

    def now_all(self):
        return [(s, self.cnt[s]) for s in self.ALL_SEMS if self.cnt[s] > 0]


def _j_first(j):
    return j == 0


def set_tokens(a, b):
    return (a or []) + (b or [])


def _j_set(j):
    return j


def build_program(upto="full"):
    nc = bass.Bass("TRN2", target_bir_lowering=False)
    kb = KB()

    x0_d = nc.dram_tensor("x0", [NQT * 128, D], F32, kind="ExternalInput").ap()
    wqa_d = nc.dram_tensor("wqa", [D, LQ], F32, kind="ExternalInput").ap()
    qnw_d = nc.dram_tensor("qnw", [1, LQ], F32, kind="ExternalInput").ap()
    wqb_d = nc.dram_tensor("wqb", [LQ, H * DQK], F32, kind="ExternalInput").ap()
    wkr_d = nc.dram_tensor("wkr", [D, DR], F32, kind="ExternalInput").ap()
    wkva_d = nc.dram_tensor("wkva", [D, LKV], F32, kind="ExternalInput").ap()
    kvnw_d = nc.dram_tensor("kvnw", [1, LKV], F32, kind="ExternalInput").ap()
    wkvb_d = nc.dram_tensor("wkvb", [LKV, H * 256], F32, kind="ExternalInput").ap()
    wo_d = nc.dram_tensor("wo", [D, D], F32, kind="ExternalInput").ap()
    fw_d = nc.dram_tensor("fw", [D, D], F32, kind="ExternalInput").ap()
    fb_d = nc.dram_tensor("fb", [1, D], F32, kind="ExternalInput").ap()
    eye_d = nc.dram_tensor("eye", [128, 128], F32, kind="ExternalInput").ap()
    mask_d = nc.dram_tensor("maskc", [128, 512], F32, kind="ExternalInput").ap()
    cqt_d = nc.dram_tensor("cqt", [32, 1024], F32, kind="ExternalInput").ap()
    sqt_d = nc.dram_tensor("sqt", [32, 1024], F32, kind="ExternalInput").ap()
    y_d = nc.dram_tensor("y", [NQT * 128, D], F32, kind="ExternalOutput").ap()
    x1s_d = nc.dram_tensor("x1s", [NQT * 128, D], F32, kind="Internal").ap()
    cex_i = nc.dram_tensor("cex_i", [128, 4 * 1024], BF16, kind="Internal").ap()
    cex_o = nc.dram_tensor("cex_o", [4 * 128, 4 * 1024], BF16, kind="Internal").ap()
    krx_i = nc.dram_tensor("krx_i", [64, 1024], BF16, kind="Internal").ap()
    krx_o = nc.dram_tensor("krx_o", [4 * 64, 1024], BF16, kind="Internal").ap()
    x2s_d = nc.dram_tensor("x2s", [NQT * 128, D], F32, kind="Internal").ap()
    wts_d = nc.dram_tensor("wts", [D, D], BF16, kind="Internal").ap()
    o_d = nc.dram_tensor("o_scr", [128, H * 1024], BF16, kind="Internal").ap()

    glob = contextlib.ExitStack()

    def sbg(name, shape, dt):
        return glob.enter_context(nc.sbuf_tensor(name, shape, dt))

    cols = sbg("cols", [128, 32], F32)
    rows = sbg("rows", [1, 512], F32)
    m_cols = sbg("m_cols", [128, 8], F32)
    ones_f = sbg("ones_f", [1, 128], F32)
    ones_c = sbg("ones_c", [128, 1], F32)
    id_b = sbg("id_b", [128, 128], BF16)
    hb = sbg("hb", [128, D], BF16)

    def C(i, n=1):
        return cols[:, i:i + n]

    psS1 = glob.enter_context(nc.psum_tensor("psS1", [128, 1024], F32))
    psS2 = glob.enter_context(nc.psum_tensor("psS2", [128, 1024], F32))
    psA = psS1[:, 0:512]
    psB = psS1[:, 512:1024]
    psC = glob.enter_context(nc.psum_tensor("psC", [128, 512], F32))
    psD = glob.enter_context(nc.psum_tensor("psD", [128, 512], F32))
    psE = glob.enter_context(nc.psum_tensor("psE", [128, 512], F32))
    psF = glob.enter_context(nc.psum_tensor("psF", [128, 512], F32))
    ps_pool = [psA, psB]
    ps_o = [psC, psD]
    ps_gate = {0: [], 1: []}

    t_id = kb.dma("gpsimd", id_b[:, :], eye_d)
    t_m1 = kb.op("vector", lambda e: e.memset(ones_f[:, :], 1.0))
    t_m2 = kb.op("vector", lambda e: e.memset(ones_c[:, :], 1.0))
    t_m3 = kb.op("vector", lambda e: e.memset(C(ACC), 0.0))

    stA = contextlib.ExitStack()
    w_tT = stA.enter_context(nc.sbuf_tensor("w_tT", [128, 16 * D], BF16))
    p0 = contextlib.ExitStack()
    wtile = [p0.enter_context(nc.sbuf_tensor(f"wtile{i}", [128, D], F32))
             for i in range(4)]
    gtl2 = [p0.enter_context(nc.sbuf_tensor(f"gtl{i}", [128, D], F32))
            for i in range(2)]
    wt_tm = [p0.enter_context(nc.sbuf_tensor(f"wt_tm{i}", [128, D], BF16))
             for i in range(2)]

    rd_tok = {}
    for oc in range(16):
        w = wtile[oc % 4]
        ld = kb.dma("scalar", w[:, :], fw_d[oc * 128:(oc + 1) * 128, :],
                    waits=[rd_tok.get(oc - 4)])
        kb.op("vector", lambda e, _w=w: e.reduce_sum(
            out=C(TMP), in_=_w[:, :], axis=AX, apply_absolute_value=True), waits=[ld])
        rd_tok[oc] = kb.op("vector", lambda e: e.tensor_tensor(
            out=C(ACC), in0=C(ACC), in1=C(TMP), op=ADD), waits=[t_m3])
    t_tot = kb.op("tensor", lambda e: e.matmul(
        psE[0:1, 0:1], C(ACC), ones_c[:, :], start=True, stop=True),
        waits=[rd_tok[15], t_m2, t_m1])
    kb.op("vector", lambda e: e.tensor_scalar(
        out=rows[0:1, 0:1], in0=psE[0:1, 0:1], scalar1=0.5 / float(D * D),
        scalar2=0.5e-10, op0=MUL, op1=ADD), waits=[t_tot])
    t_rw = kb.op("vector", lambda e: e.tensor_scalar(
        out=rows[0:1, 1:2], in0=psE[0:1, 0:1], scalar1=0.5 / float(D * D),
        scalar2=None, op0=MUL))
    t_bc = kb.op("tensor", lambda e: e.matmul(
        psE[:, 2:4], ones_f[:, :], rows[0:1, 0:2], start=True, stop=True),
        waits=[t_rw, t_m1])
    kb.op("vector", lambda e: e.tensor_copy(out=C(TCOL), in_=psE[:, 2:3]), waits=[t_bc])
    t_cc = kb.op("vector", lambda e: e.tensor_copy(out=C(CCOL), in_=psE[:, 3:4]))
    t_nt = kb.op("vector", lambda e: e.tensor_scalar(
        out=C(NTCOL), in0=C(TCOL), scalar1=-1.0, scalar2=None, op0=MUL))

    qz_tok = {}
    trl_tok = {}
    cp_hist = []
    for oc in range(16):
        w = wtile[oc % 4]
        ld = kb.dma("scalar", w[:, :], fw_d[oc * 128:(oc + 1) * 128, :],
                    waits=[qz_tok.get(oc - 4, t_nt)])
        wt = wt_tm[oc % 2]
        gtl = gtl2[oc % 2]
        kb.op("vector", lambda e, _w=w, _g=gtl: e.tensor_scalar(
            out=_g[:, :], in0=_w[:, :], scalar1=C(NTCOL), scalar2=None, op0=LT),
            waits=[ld, t_nt, qz_tok.get(oc - 2)], relax=True)
        tq2 = kb.op("vector", lambda e, _w=w, _wt=wt, _g=gtl: e.scalar_tensor_tensor(
            out=_wt[:, :], in0=_w[:, :], scalar=C(TCOL), in1=_g[:, :], op0=GT,
            op1=SUB), waits=[trl_tok.get(oc - 2)])
        qz_tok[oc] = tq2
        trl_tok[oc] = kb.dma(
            "scalar",
            w_tT[:, :].rearrange("p (ic f) -> p ic f", ic=16)[:, :,
                                                              oc * 128:(oc + 1) * 128],
            wt[:, :].rearrange("p (ic c) -> p ic c", ic=16),
            waits=[tq2], transpose=True)
    wtT_snap = kb.now("t0", "t1", "t2", "t3")
    t_wtT_ready = wtT_snap[0]
    for ic in range(16):
        kb.dma("gpsimd", wts_d[ic * 128:(ic + 1) * 128, :],
               w_tT[:, ic * D:(ic + 1) * D], waits=wtT_snap)

    prev = None
    for g in range(4):
        ldb = kb.dma("sync", rows[0:1, 0:512], fb_d[0:1, g * 512:(g + 1) * 512],
                     waits=[prev] + ([t_bc, t_rw] if g == 0 else []))
        th = kb.op("vector", lambda e: e.tensor_scalar(
            out=rows[0:1, 0:512], in0=rows[0:1, 0:512], scalar1=0.5, scalar2=None,
            op0=MUL), waits=[ldb])
        tb = kb.op("tensor", lambda e: e.matmul(
            psE[:, 0:512], ones_f[:, :], rows[0:1, 0:512], start=True, stop=True),
            waits=[th])
        prev = kb.op("vector", lambda e, _g=g: e.tensor_copy(
            out=hb[:, _g * 512:(_g + 1) * 512], in_=psE[:, 0:512]), waits=[tb])
    t_hb = prev

    def ffn_phase(stk, name, src_fn, src_tok, dst_fn, ntiles, out_sem, wtT_buf,
                  ready, load_ready=()):
        x0t = [stk.enter_context(nc.sbuf_tensor(f"{name}x{i}", [128, D], F32))
               for i in range(2)]
        stage2 = [stk.enter_context(nc.sbuf_tensor(f"{name}st{i}", [128, D], F32))
                  for i in range(2)]
        z_tm2 = [stk.enter_context(nc.sbuf_tensor(f"{name}zt{i}", [128, D], BF16))
                 for i in range(2)]
        z_fm2 = [stk.enter_context(nc.sbuf_tensor(f"{name}zf{i}", [128, 16 * 128],
                                                  BF16)) for i in range(2)]
        x1t2 = [stk.enter_context(nc.sbuf_tensor(f"{name}o{i}", [128, D], F32))
                for i in range(2)]
        last_res = {}
        last_sq = {}
        last_store = {}
        last_mm = {}
        last_tr = {}
        stores = []
        for t in range(ntiles):
            xb = x0t[t % 2]
            stage = stage2[t % 2]
            z_tm = z_tm2[t % 2]
            z_fm = z_fm2[t % 2]
            x1t = x1t2[t % 2]
            lw = [last_res.get(t - 2), last_sq.get(t - 2), src_tok(t)]
            if t < 2:
                lw += list(load_ready)
            ld = kb.dma("sync", xb[:, :], src_fn(t), waits=lw)
            kb.op("vector", lambda e, _x=xb: e.reduce_sum(
                out=C(SUM), in_=_x[:, :], axis=AX), waits=[ld])
            ts_sq = kb.op("scalar", lambda e, _x=xb, _st=stage: e.activation(
                out=_st[:, :], in_=_x[:, :], func=AF.Square, accum_out=C(SQ)),
                waits=[ld, last_res.get(t - 2)])
            last_sq[t] = ts_sq
            kb.op("vector", lambda e: e.tensor_scalar(
                out=C(MEAN), in0=C(SUM), scalar1=1.0 / D, scalar2=None, op0=MUL))
            kb.op("vector", lambda e: e.tensor_scalar(
                out=C(M2), in0=C(MEAN), scalar1=C(MEAN), scalar2=None, op0=MUL))
            kb.op("vector", lambda e: e.tensor_scalar(
                out=C(VAR), in0=C(SQ), scalar1=1.0 / D, scalar2=EPS_LN,
                op0=MUL, op1=ADD), waits=[ts_sq])
            t_var = kb.op("vector", lambda e: e.tensor_tensor(
                out=C(VAR), in0=C(VAR), in1=C(M2), op=SUB))
            t_sd = kb.op("scalar", lambda e: e.activation(
                out=C(SD), in_=C(VAR), func=AF.Sqrt), waits=[t_var])
            kb.op("vector", lambda e: e.reciprocal(out=C(RSTD), in_=C(SD)), waits=[t_sd])
            t_rc = kb.op("vector", lambda e: e.tensor_scalar(
                out=C(RC), in0=C(RSTD), scalar1=C(CCOL), scalar2=None, op0=MUL))
            kb.op("vector", lambda e, _x=xb, _st=stage: e.tensor_scalar(
                out=_st[:, :], in0=_x[:, :], scalar1=C(MEAN), scalar2=None, op0=SUB),
                waits=[ts_sq])
            t_z = kb.op("vector", lambda e, _zt=z_tm, _st=stage: e.tensor_scalar(
                out=_zt[:, :], in0=_st[:, :], scalar1=C(RC), scalar2=None, op0=MUL),
                waits=[t_rc] + list(last_tr.get(t - 2) or []))
            t_ztr = kb.dma("scalar",
                           z_fm[:, :].rearrange("p (t c) -> p t c", t=16),
                           z_tm[:, :].rearrange("p (t c) -> p t c", t=16),
                           waits=[t_z, last_mm.get(t - 2)], transpose=True)
            zfm_snap = [t_ztr]
            last_tr[t] = zfm_snap
            for og in range(4):
                par = og % 2
                ps = ps_pool[par]
                mw = list(zfm_snap) + list(ready) + ps_gate[par]
                tok_mm = None
                for ic in range(16):
                    tok_mm = kb.op("tensor", lambda e, _ic=ic, _og=og, _ps=ps,
                                   _zf=z_fm: e.matmul(
                        _ps[:, :], _zf[:, _ic * 128:(_ic + 1) * 128],
                        wtT_buf[:, _ic * D + _og * 512: _ic * D + (_og + 1) * 512],
                        start=(_ic == 0), stop=(_ic == 15)),
                        waits=mw if ic == 0 else [])
                last_mm[t] = tok_mm
                ew = [tok_mm, t_hb]
                if og == 0:
                    ew.append(last_store.get(t - 2))
                t_e1 = kb.op("vector", lambda e, _og=og, _ps=ps,
                             _st=stage: e.scalar_tensor_tensor(
                    out=_st[:, _og * 512:(_og + 1) * 512], in0=_ps[:, :], scalar=1.0,
                    in1=hb[:, _og * 512:(_og + 1) * 512], op0=MUL, op1=ADD), waits=ew,
                    relax=(og > 0))
                ps_gate[par] = [t_e1]
                t_e2 = kb.op("vector", lambda e, _og=og, _x=xb, _st=stage,
                             _xo=x1t: e.tensor_tensor(
                    out=_xo[:, _og * 512:(_og + 1) * 512],
                    in0=_st[:, _og * 512:(_og + 1) * 512],
                    in1=_x[:, _og * 512:(_og + 1) * 512], op=ADD))
                last_res[t] = t_e2
            st = kb.dma("scalar", dst_fn(t), x1t[:, :], waits=[last_res[t]],
                        sem=out_sem)
            last_store[t] = st
            stores.append(st)
        return stores

    if upto == "p0":
        dbg = kb.dma("gpsimd", y_d[0:128, :], hb[:, :], waits=[t_hb, t_wtT_ready],
                     sem="outd")
        kb.raw("sync", lambda e: e.wait_ge(kb.sems["outd"], 16))
        return _assemble(nc, kb, glob)
    p1a = contextlib.ExitStack()
    p1a_stores = ffn_phase(
        p1a, "fa", lambda t: x0_d[t * 128:(t + 1) * 128, :], lambda t: None,
        lambda t: x1s_d[t * 128:(t + 1) * 128, :], NQT, None,
        w_tT, wtT_snap + [t_cc, t_hb])
    bar1 = kb.now_all()
    if upto == "p1a":
        kb.dma("sync", y_d[0:128, :], x1s_d[0:128, :], waits=bar1, sem="outd")
        kb.raw("sync", lambda e: e.wait_ge(kb.sems["outd"], 16))
        return _assemble(nc, kb, glob)
    p1a.close()
    p0.close()
    stA.close()

    res = contextlib.ExitStack()
    ckv_fm = res.enter_context(nc.sbuf_tensor("ckv_fm", [128, 4 * T], BF16))
    krot_fm = res.enter_context(nc.sbuf_tensor("krot_fm", [128, T], BF16))
    cq_fm = res.enter_context(nc.sbuf_tensor("cq_fm", [128, 12 * 1024], BF16))

    p1b = contextlib.ExitStack()
    wqa_s = p1b.enter_context(nc.sbuf_tensor("wqa_s", [128, 16 * LQ], BF16))
    wkva_s = p1b.enter_context(nc.sbuf_tensor("wkva_s", [128, 16 * LKV], BF16))
    wkr_s = p1b.enter_context(nc.sbuf_tensor("wkr_s", [128, 16 * DR], BF16))
    x1r = [p1b.enter_context(nc.sbuf_tensor(f"x1r{i}", [128, D], F32))
           for i in range(2)]
    stage1 = p1b.enter_context(nc.sbuf_tensor("stage1", [128, D], F32))
    h_tm2 = [p1b.enter_context(nc.sbuf_tensor(f"h_tm{i}", [128, D], BF16))
             for i in range(2)]
    h_fm2 = [p1b.enter_context(nc.sbuf_tensor(f"h_fm{i}", [128, 16 * 128], BF16))
             for i in range(2)]
    cqs = p1b.enter_context(nc.sbuf_tensor("cqs", [128, LQ], F32))
    cqn_tm = p1b.enter_context(nc.sbuf_tensor("cqn_tm", [128, LQ], BF16))
    ckvn_tm = p1b.enter_context(nc.sbuf_tensor("ckvn_tm", [128, LKV], BF16))
    wq_t = p1b.enter_context(nc.sbuf_tensor("wq_t", [128, LQ], F32))
    wkv_t = p1b.enter_context(nc.sbuf_tensor("wkv_t", [128, LKV], F32))
    cksk = p1b.enter_context(nc.sbuf_tensor("cksk", [64, 1024], BF16))
    ckv_own = p1b.enter_context(nc.sbuf_tensor("ckv_own", [128, 4 * 1024], BF16))
    krot_own = p1b.enter_context(nc.sbuf_tensor("krot_own", [64, 1024], BF16))

    wl_last = None
    for c0 in range(16):
        kb.dma("gpsimd", wqa_s[:, c0 * LQ:(c0 + 1) * LQ],
               wqa_d[c0 * 128:(c0 + 1) * 128, :], waits=bar1 if c0 == 0 else [])
        kb.dma("gpsimd", wkva_s[:, c0 * LKV:(c0 + 1) * LKV],
               wkva_d[c0 * 128:(c0 + 1) * 128, :])
        wl_last = kb.dma("gpsimd", wkr_s[:, c0 * DR:(c0 + 1) * DR],
                         wkr_d[c0 * 128:(c0 + 1) * 128, :])
    kb.dma("gpsimd", cksk[0:32, :], cqt_d)
    kb.dma("gpsimd", cksk[32:64, :], sqt_d)
    wl_toks = kb.now("g0", "g1", "g2", "g3")

    prev = None
    for g in range(3):
        ldq = kb.dma("sync", rows[0:1, 0:512], qnw_d[0:1, g * 512:(g + 1) * 512],
                     waits=bar1 + [prev])
        tb = kb.op("tensor", lambda e: e.matmul(
            psE[:, 0:512], ones_f[:, :], rows[0:1, 0:512], start=True, stop=True),
            waits=[ldq] + bar1)
        prev = kb.op("vector", lambda e, _g=g: e.tensor_copy(
            out=wq_t[:, _g * 512:(_g + 1) * 512], in_=psE[:, 0:512]),
            waits=[tb] + bar1)
    ldk = kb.dma("sync", rows[0:1, 0:512], kvnw_d[0:1, 0:512], waits=[prev])
    tb = kb.op("tensor", lambda e: e.matmul(
        psE[:, 0:512], ones_f[:, :], rows[0:1, 0:512], start=True, stop=True),
        waits=[ldk])
    t_wnorm = kb.op("vector", lambda e: e.tensor_copy(
        out=wkv_t[:, 0:512], in_=psE[:, 0:512]), waits=[tb])

    if upto == "p1bw":
        kb.dma("gpsimd", y_d[0:128, 0:D], wqa_s[:, 0:D], waits=kb.now_all(),
               sem="outd")
        kb.raw("sync", lambda e: e.wait_ge(kb.sems["outd"], 16))
        return _assemble(nc, kb, glob)

    lnh_tok = {}
    lnsq_tok = {}
    htr_tok = {}
    hread_tok = {}

    def emit_ln(j):
        b = j % 2
        xb = x1r[b]
        lw = [lnh_tok.get(j - 2), lnsq_tok.get(j - 2), p1a_stores[j]]
        if j < 2:
            lw += bar1
        ld = kb.dma("sync", xb[:, :], x1s_d[j * 128:(j + 1) * 128, :], waits=lw)
        kb.op("vector", lambda e, _x=xb: e.reduce_sum(
            out=C(SUM), in_=_x[:, :], axis=AX), waits=[ld])
        t_nm = kb.op("vector", lambda e: e.tensor_scalar(
            out=C(MEAN), in0=C(SUM), scalar1=-1.0 / D, scalar2=None, op0=MUL))
        ts_sq = kb.op("scalar", lambda e, _x=xb, _ht=h_tm2[b]: e.activation(
            out=_ht[:, :], in_=_x[:, :], func=AF.Square, bias=C(MEAN),
            accum_out=C(SQ)), waits=[ld, t_nm, htr_tok.get(j - 2)])
        lnsq_tok[j] = ts_sq
        t_var = kb.op("vector", lambda e: e.tensor_scalar(
            out=C(VAR), in0=C(SQ), scalar1=1.0 / D, scalar2=EPS_LN,
            op0=MUL, op1=ADD), waits=[ts_sq])
        t_sd = kb.op("scalar", lambda e: e.activation(
            out=C(VAR), in_=C(VAR), func=AF.Sqrt), waits=[t_var])
        t_rstd = kb.op("vector", lambda e: e.reciprocal(out=C(RSTD), in_=C(VAR)),
                       waits=[t_sd])
        kb.op("vector", lambda e, _x=xb: e.tensor_scalar(
            out=_x[:, :], in0=_x[:, :], scalar1=C(MEAN), scalar2=None, op0=ADD))
        t_h = kb.op("vector", lambda e, _x=xb, _ht=h_tm2[b]: e.tensor_scalar(
            out=_ht[:, :], in0=_x[:, :], scalar1=C(RSTD), scalar2=None, op0=MUL))
        lnh_tok[j] = t_h
        hread_tok[j - 2] = kb.now("te")[0]
        htr_tok[j] = kb.dma("scalar",
                            h_fm2[b][:, :].rearrange("p (t c) -> p t c", t=16),
                            h_tm2[b][:, :].rearrange("p (t c) -> p t c", t=16),
                            waits=[t_h, hread_tok.get(j - 2)], transpose=True)

    emit_ln(0)
    last_ckvtr = []
    cqtr = []
    for t in range(NQT):
        own = True
        i_own = t
        if t + 1 < NQT:
            emit_ln(t + 1)
        h_fm = h_fm2[t % 2]
        hfm_snap = [htr_tok[t]]

        mw = list(hfm_snap) + wl_toks + ps_gate[0]
        tok_mm = None
        for ic in range(16):
            tok_mm = kb.op("tensor", lambda e, _ic=ic, _hf=h_fm: e.matmul(
                psA[:, :], _hf[:, _ic * 128:(_ic + 1) * 128],
                wkva_s[:, _ic * LKV:(_ic + 1) * LKV],
                start=(_ic == 0), stop=(_ic == 15)), waits=mw if ic == 0 else [])
        t_kvc = kb.op("vector", lambda e: e.tensor_copy(
            out=stage1[:, 0:512], in_=psA[:, :]), waits=[tok_mm])
        ps_gate[0] = [t_kvc]
        kb.op("vector", lambda e: e.tensor_tensor(
            out=stage1[:, 512:1024], in0=stage1[:, 0:512], in1=stage1[:, 0:512],
            op=MUL))
        t_sqkv = kb.op("vector", lambda e: e.reduce_sum(
            out=C(SQ2), in_=stage1[:, 512:1024], axis=AX))
        t_ms = kb.op("vector", lambda e: e.tensor_scalar(
            out=C(SD), in0=C(SQ2), scalar1=1.0 / LKV, scalar2=EPS_RMS,
            op0=MUL, op1=ADD), waits=[t_sqkv])
        t_sd2 = kb.op("scalar", lambda e: e.activation(
            out=C(SD), in_=C(SD), func=AF.Sqrt), waits=[t_ms])
        t_rr = kb.op("vector", lambda e: e.reciprocal(out=C(RC), in_=C(SD)),
                     waits=[t_sd2])
        t_n1 = kb.op("vector", lambda e: e.tensor_scalar(
            out=stage1[:, 0:512], in0=stage1[:, 0:512], scalar1=C(RC), scalar2=None,
            op0=MUL), waits=[t_rr])
        t_ckv = kb.op("vector", lambda e: e.tensor_tensor(
            out=ckvn_tm[:, :], in0=stage1[:, 0:512], in1=wkv_t[:, :], op=MUL),
            waits=[t_wnorm] + last_ckvtr)
        t_ckvtr = kb.dma(
            "scalar",
            ckv_own[:, :].rearrange("p (ci f) -> p ci f", ci=4)[:, :,
                                                               t * 128:(t + 1) * 128],
            ckvn_tm[:, :].rearrange("p (ci c) -> p ci c", ci=4),
            waits=[t_ckv] + last_ckvtr, transpose=True)
        last_ckvtr = [t_ckvtr]

        mw = list(hfm_snap) + wl_toks + ps_gate[1]
        tok_kr = None
        for ic in range(16):
            tok_kr = kb.op("tensor", lambda e, _ic=ic, _hf=h_fm: e.matmul(
                psB[0:64, 0:128], wkr_s[:, _ic * DR:(_ic + 1) * DR],
                _hf[:, _ic * 128:(_ic + 1) * 128],
                start=(_ic == 0), stop=(_ic == 15)), waits=mw if ic == 0 else [])
        ks = slice(t * 128, (t + 1) * 128)
        kb.op("vector", lambda e, _ks=ks: e.tensor_tensor(
            out=stage1[0:32, 0:128], in0=psB[0:32, 0:128], in1=cksk[0:32, _ks], op=MUL),
            waits=[tok_kr] + wl_toks)
        kb.op("vector", lambda e, _ks=ks: e.tensor_tensor(
            out=stage1[0:32, 128:256], in0=psB[32:64, 0:128], in1=cksk[32:64, _ks], op=MUL))
        kb.op("vector", lambda e, _ks=ks: e.tensor_tensor(
            out=krot_own[0:32, _ks], in0=stage1[0:32, 0:128],
            in1=stage1[0:32, 128:256], op=SUB))
        kb.op("vector", lambda e, _ks=ks: e.tensor_tensor(
            out=stage1[0:32, 0:128], in0=psB[32:64, 0:128], in1=cksk[0:32, _ks], op=MUL))
        t_r5 = kb.op("vector", lambda e, _ks=ks: e.tensor_tensor(
            out=stage1[0:32, 128:256], in0=psB[0:32, 0:128], in1=cksk[32:64, _ks], op=MUL))
        t_r6 = kb.op("vector", lambda e, _ks=ks: e.tensor_tensor(
            out=krot_own[32:64, _ks], in0=stage1[0:32, 0:128],
            in1=stage1[0:32, 128:256], op=ADD))
        ps_gate[1] = [t_r5]

        if own:
            skip_tr = upto.startswith("p1bq")
            cp_q = []
            for og in range(3):
                par = og % 2
                ps = ps_pool[par]
                mw = list(hfm_snap) + ps_gate[par]
                tok_q = None
                for ic in range(16):
                    tok_q = kb.op("tensor", lambda e, _ic=ic, _og=og, _ps=ps,
                                  _hf=h_fm: e.matmul(
                        _ps[:, :], _hf[:, _ic * 128:(_ic + 1) * 128],
                        wqa_s[:, _ic * LQ + _og * 512: _ic * LQ + (_og + 1) * 512],
                        start=(_ic == 0), stop=(_ic == 15)),
                        waits=mw if ic == 0 else [])
                t_cp = kb.op("vector", lambda e, _og=og, _ps=ps: e.tensor_copy(
                    out=cqs[:, _og * 512:(_og + 1) * 512], in_=_ps[:, :]),
                    waits=[tok_q])
                ps_gate[par] = [t_cp]
                cp_q.append(t_cp)
            kb.op("vector", lambda e: e.tensor_tensor(
                out=stage1[:, 512:512 + LQ], in0=cqs[:, :], in1=cqs[:, :], op=MUL),
                waits=[cp_q[2]])
            kb.op("vector", lambda e: e.reduce_sum(
                out=C(SQ2), in_=stage1[:, 512:512 + LQ], axis=AX))
            t_msq = kb.op("vector", lambda e: e.tensor_scalar(
                out=C(SD), in0=C(SQ2), scalar1=1.0 / LQ, scalar2=EPS_RMS,
                op0=MUL, op1=ADD))
            t_sdq = kb.op("scalar", lambda e: e.activation(
                out=C(SD), in_=C(SD), func=AF.Sqrt), waits=[t_msq])
            t_rq = kb.op("vector", lambda e: e.reciprocal(out=C(RC), in_=C(SD)),
                         waits=[t_sdq])
            t_cqn = None
            for og in range(3):
                kb.op("vector", lambda e, _og=og: e.tensor_scalar(
                    out=stage1[:, _og * 512:(_og + 1) * 512],
                    in0=cqs[:, _og * 512:(_og + 1) * 512],
                    scalar1=C(RC), scalar2=None, op0=MUL),
                    waits=[t_rq, cp_q[og]])
                t_cqn = kb.op("vector", lambda e, _og=og: e.tensor_tensor(
                    out=cqn_tm[:, _og * 512:(_og + 1) * 512],
                    in0=stage1[:, _og * 512:(_og + 1) * 512],
                    in1=wq_t[:, _og * 512:(_og + 1) * 512], op=MUL),
                    waits=[t_wnorm] + (cqtr if og == 0 else []))
            if not skip_tr:
                cqtr = [kb.dma(
                    "scalar",
                    cq_fm[:, :].rearrange("p (ci f) -> p ci f", ci=12)[
                        :, :, i_own * 128:(i_own + 1) * 128],
                    cqn_tm[:, :].rearrange("p (ci c) -> p ci c", ci=12),
                    waits=[t_cqn], transpose=True)]
        if upto.startswith("p1b") and upto[3:].isdigit() and t == int(upto[3:]) - 1:
            break
        if upto.startswith("p1bq") and t == 3:
            break

    exw = kb.now("ve", "se", "t0", "t1", "t2", "t3")
    st_ckv = kb.dma("scalar", cex_i, ckv_own[:, :], waits=exw)
    st_kr = kb.dma("scalar", krx_i, krot_own[:, :], waits=exw)
    rg = [[0, 1, 2, 3], [4, 5, 6, 7]]
    kb.raw("gpsimd", lambda e, _a=st_ckv, _b=st_kr: e.wait_ge(
        kb.sems[_a[0]], _a[1]))
    kb.raw("gpsimd", lambda e, _b=st_kr: e.wait_ge(kb.sems[_b[0]], _b[1]))
    kb.cnt["gc"] += 2

    def _coll(e, _rg=rg):
        e.collective_compute("AllGather", AluOpType.bypass, replica_groups=_rg,
                             ins=[cex_i], outs=[cex_o]).then_inc(kb.sems["gc"], 1)
        e.collective_compute("AllGather", AluOpType.bypass, replica_groups=_rg,
                             ins=[krx_i], outs=[krx_o]).then_inc(kb.sems["gc"], 1)
    kb.raw("gpsimd", _coll)
    coll_tok = ("gc", kb.cnt["gc"])
    gl = []
    for b in range(4):
        for ci in range(4):
            gl.append(kb.dma(
                "sync",
                ckv_fm[:, ci * T:(ci + 1) * T].rearrange(
                    "p (t f) -> p t f", f=512)[:, :, b * 128:(b + 1) * 128],
                cex_o[b * 128:(b + 1) * 128, ci * 1024:(ci + 1) * 1024].rearrange(
                    "p (t c) -> p t c", t=8),
                waits=[coll_tok]))
        gl.append(kb.dma(
            "sync",
            krot_fm[0:64, :].rearrange("p (t f) -> p t f", f=512)[
                :, :, b * 128:(b + 1) * 128],
            krx_o[b * 64:(b + 1) * 64, :].rearrange("p (t c) -> p t c", t=8),
            waits=[coll_tok]))
    gl_toks = kb.now("s0", "s1", "s2", "s3")
    kb.op("vector", lambda e: e.memset(krot_fm[64:128, :], 0.0))

    bar2 = kb.now_all()
    if upto.startswith("p1b") and upto != "p1bw":
        kb.dma("gpsimd", y_d[0:128, 0:D], ckv_fm[:, 0:D],
               waits=bar2, sem="outd")
        kb.raw("sync", lambda e: e.wait_ge(kb.sems["outd"], 16))
        return _assemble(nc, kb, glob)
    p1b.close()

    p2 = contextlib.ExitStack()
    wqb_s = [p2.enter_context(nc.sbuf_tensor(f"wqb{i}", [128, 12 * DQK], BF16))
             for i in range(2)]
    wkvb_s = [p2.enter_context(nc.sbuf_tensor(f"wkvb{i}", [128, 4 * 256], BF16))
              for i in range(2)]
    qfa = p2.enter_context(nc.sbuf_tensor("qfa", [128, 1024], BF16))
    qfb = p2.enter_context(nc.sbuf_tensor("qfb", [128, 1024], BF16))
    kp_fm = p2.enter_context(nc.sbuf_tensor("kp_fm", [128, T], BF16))
    v_tm = p2.enter_context(nc.sbuf_tensor("v_tm", [128, 32 * 128], BF16))
    S_2 = [p2.enter_context(nc.sbuf_tensor(f"S_{i}", [128, T], F32))
           for i in range(2)]
    P_s = [p2.enter_context(nc.sbuf_tensor(f"P{j}", [128, T], BF16)) for j in range(4)]
    PT_n = p2.enter_context(nc.sbuf_tensor("PT_n", [128, 4 * 2048], BF16))
    v_fm = p2.enter_context(nc.sbuf_tensor("v_fm", [128, 16 * 128], BF16))
    o_hd = [p2.enter_context(nc.sbuf_tensor(f"o_hd{i}", [128, 1024], BF16))
            for i in range(2)]
    rsumb = p2.enter_context(nc.sbuf_tensor("rsumb", [128, 512], F32))
    rs4 = p2.enter_context(nc.sbuf_tensor("rs4", [128, 8], F32))
    mask_b = p2.enter_context(nc.sbuf_tensor("mask_b", [128, 512], BF16))
    id_f = p2.enter_context(nc.sbuf_tensor("id_f", [128, 128], F32))
    cqsq = p2.enter_context(nc.sbuf_tensor("cqsq", [64, 1024], BF16))
    cqsq2 = p2.enter_context(nc.sbuf_tensor("cqsq2", [64, 1024], BF16))
    rope_t = p2.enter_context(nc.sbuf_tensor("rope_t", [64, 1024], F32))
    qfb_f = p2.enter_context(nc.sbuf_tensor("qfb_f", [64, 1024], F32))

    t_mk = kb.dma("gpsimd", mask_b[:, :], mask_d, waits=bar2)
    t_idf = kb.dma("sync", id_f[:, :], eye_d, waits=bar2)
    kb.op("vector", lambda e: e.memset(qfb[64:128, :], 0.0))
    t_cqs = kb.dma("gpsimd", cqsq[0:32, :], cqt_d)
    kb.dma("gpsimd", cqsq[32:64, :], sqt_d)
    kb.dma("gpsimd", cqsq2[32:64, :], cqt_d)
    t_qt = kb.dma("gpsimd", cqsq2[0:32, :], sqt_d)

    def load_head_w(h, waits):
        for ic in range(12):
            kb.dma("gpsimd", wqb_s[h % 2][:, ic * DQK:(ic + 1) * DQK],
                   wqb_d[ic * 128:(ic + 1) * 128, h * DQK:(h + 1) * DQK],
                   waits=waits if ic == 0 else [])
        for ci in range(4):
            kb.dma("gpsimd", wkvb_s[h % 2][:, ci * 256:(ci + 1) * 256],
                   wkvb_d[ci * 128:(ci + 1) * 128, h * 256:(h + 1) * 256])
        return kb.now("g0", "g1", "g2", "g3")

    head_wload = {0: load_head_w(0, bar2 + [t_qt])}
    head_te_end = {}
    exp_gate = {0: None, 1: None}
    oev_gate = {0: None, 1: None}
    pslot_gate = {j: None for j in range(4)}
    pt_gate = None
    s_gate = {0: [], 1: [], 2: [], 3: []}
    pv_prev_half = None
    t_ptz = kb.op("vector", lambda e: e.memset(PT_n[:, :], 0.0), waits=bar2)
    pt_zero = {j: t_ptz for j in range(4)}
    ost_gate = {(p, g): None for p in range(2) for g in range(2)}
    v_gate = []
    prev_rw4 = None
    prev_tr4 = None
    prev_rsb = None
    prev_oev = None

    for h in range(H):
        wq = wqb_s[h % 2]
        wkv = wkvb_s[h % 2]
        wld = head_wload[h]
        if h + 1 < H:
            head_wload[h + 1] = load_head_w(h + 1, [head_te_end.get(h - 1)])
        te_prev = head_te_end.get(h - 1)
        t_qfa = {}
        for qs in range(2):
            par = qs % 2
            ps = ps_pool[par]
            mw = wld + ps_gate[par]
            tok = None
            for ic in range(12):
                tok = kb.op("tensor", lambda e, _ic=ic, _qs=qs, _ps=ps, _wq=wq: e.matmul(
                    _ps[:, :], _wq[:, _ic * DQK:_ic * DQK + 128],
                    cq_fm[:, _ic * 1024 + _qs * 512: _ic * 1024 + (_qs + 1) * 512],
                    start=(_ic == 0), stop=(_ic == 11)),
                    waits=mw if ic == 0 else [])
            t_qfa[qs] = kb.op("vector", lambda e, _qs=qs, _ps=ps: e.tensor_copy(
                out=qfa[:, _qs * 512:(_qs + 1) * 512], in_=_ps[:, :]),
                waits=[tok, te_prev])
            ps_gate[par] = [t_qfa[qs]]
        t_qf_last = None
        for qs in range(2):
            par = qs % 2
            ps = ps_pool[par]
            mw = ps_gate[par]
            tok = None
            for ic in range(12):
                tok = kb.op("tensor", lambda e, _ic=ic, _qs=qs, _ps=ps, _wq=wq: e.matmul(
                    _ps[0:64, :], _wq[:, _ic * DQK + 128:(_ic + 1) * DQK],
                    cq_fm[:, _ic * 1024 + _qs * 512: _ic * 1024 + (_qs + 1) * 512],
                    start=(_ic == 0), stop=(_ic == 11)),
                    waits=mw if ic == 0 else [])
            sl = slice(qs * 512, (qs + 1) * 512)
            t_qe = kb.op("vector", lambda e, _ps=ps, _qs=qs: e.tensor_copy(
                out=qfb_f[:, _qs * 512:(_qs + 1) * 512], in_=_ps[0:64, :]),
                waits=[tok])
            ps_gate[par] = [t_qe]
        kb.op("vector", lambda e: e.tensor_tensor(
            out=qfb[0:32, :], in0=qfb_f[0:32, :], in1=cqsq[0:32, :], op=MUL),
            waits=[t_qt, t_cqs])
        kb.op("vector", lambda e: e.tensor_tensor(
            out=rope_t[0:32, :], in0=qfb_f[32:64, :], in1=cqsq[32:64, :], op=MUL))
        kb.op("vector", lambda e: e.tensor_tensor(
            out=qfb[0:32, :], in0=qfb[0:32, :], in1=rope_t[0:32, :], op=SUB))
        kb.op("vector", lambda e: e.tensor_tensor(
            out=qfb[32:64, :], in0=qfb_f[32:64, :], in1=cqsq2[32:64, :], op=MUL))
        kb.op("vector", lambda e: e.tensor_tensor(
            out=rope_t[32:64, :], in0=qfb_f[0:32, :], in1=cqsq2[0:32, :], op=MUL))
        t_qf_last = kb.op("vector", lambda e: e.tensor_tensor(
            out=qfb[32:64, :], in0=qfb[32:64, :], in1=rope_t[32:64, :], op=ADD))
        t_kp = {}
        for kc8 in range(8):
            par = kc8 % 2
            ps = ps_pool[par]
            mw = wld + ps_gate[par]
            tok = None
            for ci in range(4):
                tok = kb.op("tensor", lambda e, _ci=ci, _k=kc8, _ps=ps, _w=wkv: e.matmul(
                    _ps[:, :], _w[:, _ci * 256:_ci * 256 + 128],
                    ckv_fm[:, _ci * T + _k * 512: _ci * T + (_k + 1) * 512],
                    start=(_ci == 0), stop=(_ci == 3)),
                    waits=mw if ci == 0 else [])
            t_kp[kc8] = kb.op("vector", lambda e, _k=kc8, _ps=ps: e.tensor_copy(
                out=kp_fm[:, _k * 512:(_k + 1) * 512], in_=_ps[:, :]), waits=[tok],
                relax=(kc8 > 0))
            ps_gate[par] = [t_kp[kc8]]
        t_vfm = {}
        vtr = []
        for kc8 in range(8):
            par = kc8 % 2
            ps = ps_pool[par]
            mw = list(ps_gate[par])
            if kc8 == 4:
                mw.append(vtr[0])
            tok = None
            for ci in range(4):
                tok = kb.op("tensor", lambda e, _ci=ci, _k=kc8, _ps=ps, _w=wkv: e.matmul(
                    _ps[:, :], _w[:, _ci * 256 + 128:(_ci + 1) * 256],
                    ckv_fm[:, _ci * T + _k * 512: _ci * T + (_k + 1) * 512],
                    start=(_ci == 0), stop=(_ci == 3)),
                    waits=mw if ci == 0 else [])
            t_vfm[kc8] = kb.op("vector", lambda e, _k=kc8, _ps=ps: e.tensor_copy(
                out=v_fm[:, (_k % 4) * 512:(_k % 4 + 1) * 512], in_=_ps[:, :]),
                waits=[tok] + (v_gate if kc8 == 0 else [])
                + ([vtr[0]] if kc8 == 4 else []), relax=(kc8 % 4 != 0))
            ps_gate[par] = [t_vfm[kc8]]
            if kc8 == 3 or kc8 == 7:
                half = kc8 // 4
                vtr.append(kb.dma(
                    "scalar",
                    v_tm[:, half * 2048:(half + 1) * 2048].rearrange(
                        "p (t c) -> p t c", t=16),
                    v_fm[:, :].rearrange("p (t c) -> p t c", t=16),
                    waits=[t_vfm[kc8]] + ([te_prev] if half == 0 else []),
                    transpose=True))
        v_ready = [vtr[0], vtr[1]]
        v_gate = [vtr[1]]

        grp = {g: dict(msk=[None] * 4, ea=[None] * 4, pad=[None] * 4,
                       trw=[None] * 4, t_rs=None, nkt=2048 * (g + 1) // 128)
               for g in range(2)}
        tile_st = {}
        BIAS_COLS = [BIAS, ACC]

        def emit_A(g, j):
            s_gate[0] = list(set_tokens(s_gate[0], ps_gate[0]))
            s_gate[1] = list(set_tokens(s_gate[1], ps_gate[1]))
            i = 4 * g + j
            K = 512 * (i + 1)
            qsl = slice(i * 128, (i + 1) * 128)
            Sb = S_2[i % 2]
            sc_last = None
            npair = 0
            for kc8 in range(i + 1):
                slot = kc8 % 4
                psx = psS1 if slot < 2 else psS2
                reg = psx[:, (slot % 2) * 512:(slot % 2 + 1) * 512]
                mw = [t_kp[kc8], t_qf_last] + s_gate[slot]
                kb.op("tensor", lambda e, _k=kc8, _r=reg, _q=qsl: e.matmul(
                    _r, qfa[:, _q], kp_fm[:, _k * 512:(_k + 1) * 512],
                    start=True, stop=False), waits=mw)
                tok = kb.op("tensor", lambda e, _k=kc8, _r=reg, _q=qsl: e.matmul(
                    _r, qfb[:, _q], krot_fm[:, _k * 512:(_k + 1) * 512],
                    start=False, stop=True))
                if slot % 2 == 1 or kc8 == i:
                    w = 1024 if slot % 2 == 1 else 512
                    sc_last = kb.op(
                        "vector", lambda e, _k=kc8, _S=Sb, _w=w, _n=npair,
                        _px=psx: e.tensor_scalar(
                            out=_S[:, (_k + 1) * 512 - _w:(_k + 1) * 512],
                            in0=_px[:, 0:_w],
                            scalar1=1.0, scalar2=NEG_BIG, op0=MUL, op1=MAXOP,
                            accum_out=m_cols[:, _n:_n + 1]),
                        waits=[tok, exp_gate[i % 2]], relax=(npair > 0))
                    s_gate[slot - (slot % 2)] = [sc_last]
                    s_gate[slot - (slot % 2) + 1] = [sc_last]
                    npair += 1
            ps_gate[0] = s_gate[0]
            ps_gate[1] = s_gate[1]
            bc = BIAS_COLS[i % 2]
            t_bias = kb.op("vector", lambda e, _b=bc, _n=npair: e.tensor_scalar(
                out=m_cols[:, 0:_n], in0=m_cols[:, 0:_n], scalar1=-SCALE,
                scalar2=3.0e38, op0=MUL, op1=AluOpType.min,
                accum_out=C(_b)), waits=[sc_last, exp_gate[i % 2]])
            tile_st[(g, j)] = dict(K=K, Sb=Sb, bias=t_bias, bc=bc)

        prev_sum = [None]

        def emit_B(g, j):
            i = 4 * g + j
            st = tile_st[(g, j)]
            K, Sb, bc = st["K"], st["Sb"], st["bc"]
            maxK = 2048 * (g + 1)
            Pj = P_s[j]
            pw = [st["bias"], prev_sum[0]] + (pslot_gate[j] or [])
            t_ea = None
            if i > 0:
                t_ea = kb.op("scalar", lambda e, _P=Pj, _K=K, _S=Sb,
                             _b=bc: e.activation(
                    out=_P[:, 0:_K - 512], in_=_S[:, 0:_K - 512], func=AF.Exp,
                    bias=C(_b), scale=SCALE, accum_out=C(SA)), waits=pw)
            t_ed = kb.op("scalar", lambda e, _P=Pj, _K=K, _S=Sb, _b=bc: e.activation(
                out=_P[:, _K - 512:_K], in_=_S[:, _K - 512:_K], func=AF.Exp,
                bias=C(_b), scale=SCALE), waits=pw)
            exp_gate[i % 2] = t_ed
            t_msk = kb.op("vector", lambda e, _P=Pj, _K=K: e.scalar_tensor_tensor(
                out=_P[:, _K - 512:_K], in0=_P[:, _K - 512:_K], scalar=1.0,
                in1=mask_b[:, :], op0=MUL, op1=MUL, accum_out=C(SK)),
                waits=[t_ed, t_mk])
            grp[g]["msk"][j] = t_msk
            grp[g]["ea"][j] = t_ea
            if K < maxK:
                grp[g]["pad"][j] = kb.op("vector", lambda e, _P=Pj, _K=K, _M=maxK:
                                         e.memset(_P[:, _K:_M], 0.0),
                                         waits=pslot_gate[j] or [])
            if i > 0:
                t_sum = kb.op("vector", lambda e: e.tensor_tensor(
                    out=C(SSUM), in0=C(SK), in1=C(SA), op=ADD), waits=[t_ea])
            else:
                t_sum = kb.op("vector", lambda e: e.tensor_copy(
                    out=C(SSUM), in_=C(SK)))
            prev_sum[0] = t_sum
            grp[g]["t_rs"] = kb.op("vector", lambda e, _c=i: e.reciprocal(
                out=rs4[:, _c:_c + 1], in_=C(SSUM)),
                waits=[prev_tr4] if i == 0 else [])
            if g == 0:
                used = (j + 1) * 4
                grp[0]["trw"][j] = kb.dma(
                    "scalar",
                    PT_n[:, j * 2048: j * 2048 + used * 128].rearrange(
                        "p (t c) -> p t c", t=used),
                    P_s[j][:, 0:used * 128].rearrange("p (t c) -> p t c", t=used),
                    waits=[t_msk, t_ea, grp[0]["pad"][j], pt_gate,
                           pt_zero[j]], transpose=True)
                pslot_gate[j] = [grp[0]["trw"][j]]

        tiles = [(g, j) for g in range(2) for j in range(4)]
        emit_A(0, 0)
        for idx in range(1, 8):
            emit_A(*tiles[idx])
            emit_B(*tiles[idx - 1])
        emit_B(*tiles[7])

        for g in range(2):
            st = grp[g]
            nkt = st["nkt"]
            po = ps_o[g]
            pv_tok = {}
            for half in range(nkt // 16):
                if g == 1:
                    trw = []
                    if half == 1:
                        trw_h0 = list(pslot_gate[0] + pslot_gate[1]
                                      + pslot_gate[2] + pslot_gate[3])
                        for j in range(3):
                            pt_zero[j] = kb.op(
                                "vector", lambda e, _j=j: e.memset(
                                    PT_n[:, _j * 2048 + (_j + 1) * 512:
                                         (_j + 1) * 2048], 0.0),
                                waits=([pt_gate] + trw_h0)
                                if _j_first(j) else [])
                    for j in range(4):
                        used = 16 if half == 0 else (j + 1) * 4
                        tw = [st["msk"][j], st["ea"][j], st["pad"][j], pt_gate]
                        if half == 1:
                            tw.append(pt_zero[j])
                        trw.append(kb.dma(
                            "scalar",
                            PT_n[:, j * 2048: j * 2048 + used * 128].rearrange(
                                "p (t c) -> p t c", t=used),
                            P_s[j][:, half * 2048:
                                   half * 2048 + used * 128].rearrange(
                                "p (t c) -> p t c", t=used),
                            waits=tw, transpose=True))
                        pslot_gate[j] = [trw[-1]]
                else:
                    trw = st["trw"]
                for kk in range(16):
                    kt = half * 16 + kk
                    rhs = PT_n[:, :].rearrange("p (j t c) -> p j t c",
                                               j=4, t=16)[:, :, kk, :]
                    pvw = list(trw) if kk == 0 else []
                    if kk == 0:
                        pvw += [pt_zero[j2] for j2 in range(4)]
                    if kt == 0:
                        pvw += [oev_gate[g]] + list(v_ready)
                    pv_tok[kt] = kb.op("tensor", lambda e, _kt=kt, _po=po, _r=rhs,
                                       _n=nkt: e.matmul(
                        _po[:, 0:512], v_tm[:, _kt * 128:(_kt + 1) * 128], _r,
                        start=(_kt == 0), stop=(_kt == _n - 1)),
                        waits=pvw)
                pt_gate = pv_tok[half * 16 + 15]
                pv_prev_half = pt_gate
            t_tr4 = None
            for j in range(4):
                t_tr4 = kb.op("tensor", lambda e, _j=j, _c=g * 4 + j: e.transpose(
                    psF[0:1, _j * 128:(_j + 1) * 128], rs4[:, _c:_c + 1],
                    id_f[:, :]),
                    waits=[st["t_rs"], t_idf] + ([prev_rw4] if j == 0 else []))
            prev_tr4 = t_tr4
            t_rw4 = kb.op("vector", lambda e: e.tensor_copy(
                out=rows[0:1, 0:512], in_=psF[0:1, 0:512]), waits=[t_tr4])
            prev_rw4 = t_rw4
            t_bc = None
            for j in range(4):
                t_bc = kb.op("tensor", lambda e, _j=j: e.matmul(
                    psE[:, _j * 128:(_j + 1) * 128], ones_f[:, :],
                    rows[0:1, _j * 128:(_j + 1) * 128], start=True, stop=True),
                    waits=[t_rw4] + ([prev_rsb] if j == 0 else []))
            t_rsb = kb.op("vector", lambda e: e.tensor_copy(
                out=rsumb[:, :], in_=psE[:, 0:512]), waits=[t_bc, prev_oev])
            prev_rsb = t_rsb
            t_oev = kb.op("vector", lambda e, _g=g, _po=po, _ob=o_hd[h % 2]:
                          e.tensor_tensor(
                out=_ob[:, _g * 512:(_g + 1) * 512],
                in0=_po[:, 0:512], in1=rsumb[:, :], op=MUL),
                waits=[t_rsb, pv_tok[nkt - 1], ost_gate[(h % 2, g)]])
            ost_gate[(h % 2, g)] = kb.dma(
                "sync", o_d[:, h * 1024 + g * 512: h * 1024 + (g + 1) * 512],
                o_hd[h % 2][:, g * 512:(g + 1) * 512], waits=[t_oev])
            oev_gate[g] = t_oev
            prev_oev = t_oev
        head_te_end[h] = kb.now("te")[0]

    bar3 = kb.now_all()
    if upto == "p2":
        kb.dma("gpsimd", y_d[0:128, 0:D], v_tm[:, 0:D],
               waits=bar3, sem="outd")
        kb.raw("sync", lambda e: e.wait_ge(kb.sems["outd"], 16))
        return _assemble(nc, kb, glob)
    p2.close()

    p3a = contextlib.ExitStack()
    wo_s = p3a.enter_context(nc.sbuf_tensor("wo_s", [128, 16 * D], BF16))
    x1o = [p3a.enter_context(nc.sbuf_tensor(f"x1o{i}", [128, D], F32))
           for i in range(2)]
    o_t = [p3a.enter_context(nc.sbuf_tensor(f"o_t{i}", [128, 16 * 128], BF16))
           for i in range(2)]
    x2t = [p3a.enter_context(nc.sbuf_tensor(f"x2t{i}", [128, D], F32))
           for i in range(2)]
    for ic in range(16):
        kb.dma("gpsimd", wo_s[:, ic * D:(ic + 1) * D],
               wo_d[ic * 128:(ic + 1) * 128, :],
               waits=bar3 if ic == 0 else [])
    wol_toks = kb.now("g0", "g1", "g2", "g3")
    last_r3 = {}
    last_st3 = {}
    last_omm = {}
    p3a_stores = []
    for i in range(NQT):
        xb = x1o[i % 2]
        ob = o_t[i % 2]
        x2 = x2t[i % 2]
        lw = [last_r3.get(i - 2), p1a_stores[i]]
        if i < 2:
            lw += bar3
        ld = kb.dma("sync", xb[:, :], x1s_d[i * 128:(i + 1) * 128, :], waits=lw)
        ldo = kb.dma("sync",
                     ob[:, :].rearrange("p (hc c) -> p hc c", hc=16),
                     o_d[:, :].rearrange("p (hc f) -> p hc f", hc=16)[
                         :, :, i * 128:(i + 1) * 128],
                     waits=[last_omm.get(i - 2)] + (bar3 if i < 2 else []))
        for og in range(4):
            par = og % 2
            ps = ps_pool[par]
            mw = [ld, ldo] + wol_toks + ps_gate[par]
            tok = None
            for hc in range(16):
                tok = kb.op("tensor", lambda e, _hc=hc, _og=og, _ps=ps,
                            _ob=ob: e.matmul(
                    _ps[:, :], _ob[:, _hc * 128:(_hc + 1) * 128],
                    wo_s[:, _hc * D + _og * 512: _hc * D + (_og + 1) * 512],
                    start=(_hc == 0), stop=(_hc == 15)),
                    waits=mw if hc == 0 else [])
            ew = [tok]
            if og == 0:
                ew.append(last_st3.get(i - 2))
            t_e = kb.op("vector", lambda e, _og=og, _ps=ps, _x=xb,
                        _x2=x2: e.tensor_tensor(
                out=_x2[:, _og * 512:(_og + 1) * 512], in0=_ps[:, :],
                in1=_x[:, _og * 512:(_og + 1) * 512], op=ADD), waits=ew)
            ps_gate[par] = [t_e]
            last_r3[i] = t_e
        last_omm[i] = tok
        last_st3[i] = kb.dma("scalar", x2s_d[i * 128:(i + 1) * 128, :], x2[:, :],
                             waits=[last_r3[i]])
        p3a_stores.append(last_st3[i])
    bar4 = kb.now_all()
    if upto == "p3a":
        kb.dma("sync", y_d[0:128, :], x2s_d[0:128, :], waits=bar4, sem="outd")
        kb.raw("sync", lambda e: e.wait_ge(kb.sems["outd"], 16))
        return _assemble(nc, kb, glob)
    p3a.close()
    res.close()

    p3b = contextlib.ExitStack()
    w_tT2 = p3b.enter_context(nc.sbuf_tensor("w_tT2", [128, 16 * D], BF16))
    for ic in range(16):
        kb.dma("sync", w_tT2[:, ic * D:(ic + 1) * D],
               wts_d[ic * 128:(ic + 1) * 128, :],
               waits=bar4 if ic == 0 else [])
    wrl_toks = kb.now("s0", "s1", "s2", "s3")
    ffn_phase(p3b, "fb", lambda t: x2s_d[t * 128:(t + 1) * 128, :],
              lambda t: p3a_stores[t],
              lambda t: y_d[t * 128:(t + 1) * 128, :], NQT, "outd",
              w_tT2, wrl_toks + bar4, load_ready=bar4)
    p3b.close()

    n_out = kb.cnt["outd"]
    kb.raw("sync", lambda e, _n=n_out: e.wait_ge(kb.sems["outd"], _n))
    return _assemble(nc, kb, glob)


def _assemble(nc, kb, glob):
    with contextlib.ExitStack() as semstk:
        for s in KB.ALL_SEMS:
            kb.sems[s] = semstk.enter_context(nc.semaphore(s))
        with nc.Block() as block:
            @block.tensor
            def _(eng):
                for f in kb.ops["tensor"]:
                    f(eng)

            @block.vector
            def _(eng):
                for f in kb.ops["vector"]:
                    f(eng)

            @block.scalar
            def _(eng):
                for f in kb.ops["scalar"]:
                    f(eng)

            @block.sync
            def _(eng):
                for f in kb.ops["sync"]:
                    f(eng)

            @block.gpsimd
            def _(eng):
                for f in kb.ops["gpsimd"]:
                    f(eng)
    return nc, kb


def make_in_maps(d):
    return _make_in_maps(**d)


def kernel(L0_attn_hidden_states, L0_attn_Wq_a, L0_attn_q_norm_w, L0_attn_Wq_b,
           L0_attn_Wk_rope, L0_attn_Wkv_a, L0_attn_kv_norm_w, L0_attn_Wkv_b,
           L0_attn_Wo, L0_ffn_weight, L0_ffn_bias):
    in_maps = _make_in_maps(L0_attn_hidden_states, L0_attn_Wq_a, L0_attn_q_norm_w,
                            L0_attn_Wq_b, L0_attn_Wk_rope, L0_attn_Wkv_a,
                            L0_attn_kv_norm_w, L0_attn_Wkv_b, L0_attn_Wo,
                            L0_ffn_weight, L0_ffn_bias)
    nc, _ = build_program()
    res = run_bass_kernel_spmd(nc, in_maps, core_ids=list(range(8)))

    out = np.empty((2, T, D), dtype=np.float32)
    for c in range(8):
        b, r = c // 4, c % 4
        yc = res.results[c]["y"]
        for i in range(NQT):
            g = 4 * i + r
            out[b, g * 128:(g + 1) * 128, :] = yc[i * 128:(i + 1) * 128, :]
    return out


def _make_in_maps(L0_attn_hidden_states, L0_attn_Wq_a, L0_attn_q_norm_w, L0_attn_Wq_b,
                  L0_attn_Wk_rope, L0_attn_Wkv_a, L0_attn_kv_norm_w, L0_attn_Wkv_b,
                  L0_attn_Wo, L0_ffn_weight, L0_ffn_bias):
    x = np.ascontiguousarray(L0_attn_hidden_states, dtype=np.float32)
    weights = dict(
        wqa=np.ascontiguousarray(L0_attn_Wq_a, dtype=np.float32),
        qnw=np.ascontiguousarray(L0_attn_q_norm_w, dtype=np.float32).reshape(1, LQ),
        wqb=np.ascontiguousarray(L0_attn_Wq_b, dtype=np.float32),
        wkr=np.ascontiguousarray(L0_attn_Wk_rope, dtype=np.float32),
        wkva=np.ascontiguousarray(L0_attn_Wkv_a, dtype=np.float32),
        kvnw=np.ascontiguousarray(L0_attn_kv_norm_w, dtype=np.float32).reshape(1, LKV),
        wkvb=np.ascontiguousarray(L0_attn_Wkv_b, dtype=np.float32),
        wo=np.ascontiguousarray(L0_attn_Wo, dtype=np.float32),
        fw=np.ascontiguousarray(L0_ffn_weight, dtype=np.float32),
        fb=np.ascontiguousarray(L0_ffn_bias, dtype=np.float32).reshape(1, D),
        eye=np.eye(128, dtype=np.float32),
    )
    inv_freq = 1.0 / (10000.0 ** (np.arange(0, DR, 2, dtype=np.float64) / DR))

    in_maps = []
    for c in range(8):
        b, r = c // 4, c % 4
        own_idx = ((4 * np.arange(NQT)[:, None] + r) * 128
                   + np.arange(128)[None, :]).reshape(-1)
        x0c = np.ascontiguousarray(x[b].reshape(T, D)[own_idx])
        own_rows = own_idx.astype(np.float64)
        cqt = np.cos(inv_freq[:, None] * own_rows[None, :]).astype(np.float32)
        sqt = np.sin(inv_freq[:, None] * own_rows[None, :]).astype(np.float32)
        mask = np.zeros((128, 512), dtype=np.float32)
        for p in range(4):
            if p < r:
                mask[:, p * 128:(p + 1) * 128] = 1.0
            elif p == r:
                mask[:, p * 128:(p + 1) * 128] = np.tril(
                    np.ones((128, 128), dtype=np.float32))
        m = dict(weights)
        m.update(x0=x0c, maskc=mask, cqt=cqt, sqt=sqt)
        in_maps.append(m)
    return in_maps


if __name__ == "__main__":
    nc, kb = build_program()
    print("instruction counts:",
          {k: len(v) for k, v in kb.ops.items()},
          "sem counts:", kb.cnt)


# version 29 (final)
# speedup vs baseline: 1.0603x; 1.0507x over previous
import math
import os
import contextlib
import numpy as np
import concourse.bass as bass
import concourse.mybir as mybir
from concourse.bass_utils import run_bass_kernel_spmd
from concourse.alu_op_type import AluOpType

F32 = mybir.dt.float32
BF16 = mybir.dt.bfloat16
AF = mybir.ActivationFunctionType
AX = mybir.AxisListType.X
MUL = AluOpType.mult
ADD = AluOpType.add
SUB = AluOpType.subtract
GT = AluOpType.is_gt
LT = AluOpType.is_lt
MAXOP = AluOpType.max

D = 2048
T = 4096
H = 16
DR = 64
DQK = 192
LQ = 1536
LKV = 512
NT = 32
NQT = 8
SCALE = 1.0 / math.sqrt(DQK)
EPS_LN = 1e-5
EPS_RMS = 1e-6
NEG_BIG = -3.0e38

(SUM, SQ, SQ2, SQ3, MEAN, M2, VAR, SD, RSTD, RC, TCOL, CCOL, NTCOL,
 MX, BIAS, SA, SK, SSUM, RS, ACC, TMP) = range(21)


class KB:
    ENG_SEM = {"tensor": "te", "vector": "ve", "scalar": "se"}
    Q_SEMS = {"sync": ["s0", "s1", "s2", "s3"], "scalar": ["t0", "t1", "t2", "t3"],
              "gpsimd": ["g0", "g1", "g2", "g3"]}
    ALL_SEMS = ("te", "ve", "se", "gc", "outd", "s0", "s1", "s2", "s3",
                "t0", "t1", "t2", "t3", "g0", "g1", "g2", "g3")

    def __init__(self):
        self.ops = {k: [] for k in ("tensor", "vector", "scalar", "sync", "gpsimd")}
        self.cnt = {k: 0 for k in self.ALL_SEMS}
        self.rr = {k: 0 for k in self.Q_SEMS}
        self.lastw = {k: {} for k in self.ops}
        self.sems = {}

    def _wait(self, engine, waits):
        for w in waits:
            if w is None:
                continue
            s, v = w
            if v <= 0 or self.lastw[engine].get(s, 0) >= v:
                continue
            self.lastw[engine][s] = v

            def f(eng, _s=s, _v=v):
                eng.wait_ge(self.sems[_s], _v)
            self.ops[engine].append(f)

    def op(self, engine, fn, waits=(), relax=False):
        s = self.ENG_SEM[engine]
        if engine in ("vector", "scalar") and not relax:
            self._wait(engine, list(waits) + [(s, self.cnt[s])])
        else:
            self._wait(engine, waits)
        self.cnt[s] += 1
        tok = self.cnt[s]

        def f(eng, _fn=fn, _s=s):
            _fn(eng).then_inc(self.sems[_s], 1)
        self.ops[engine].append(f)
        return (s, tok)

    def dma(self, queue, out, in_, waits=(), sem=None, transpose=False):
        if sem is None:
            pool = self.Q_SEMS[queue]
            s = pool[self.rr[queue] % len(pool)]
            self.rr[queue] += 1
        else:
            s = sem
        prev = self.cnt[s]
        self._wait(queue, list(waits) + [(s, prev)])
        self.cnt[s] = prev + 16
        tok = self.cnt[s]

        def f(eng, _o=out, _i=in_, _s=s, _t=transpose):
            eng.dma_start(out=_o, in_=_i, transpose=_t).then_inc(self.sems[_s], 16)
        self.ops[queue].append(f)
        return (s, tok)

    def raw(self, engine, fn, waits=()):
        self._wait(engine, waits)
        self.ops[engine].append(fn)

    def now(self, *sems):
        return [(s, self.cnt[s]) for s in sems]

    def now_all(self):
        return [(s, self.cnt[s]) for s in self.ALL_SEMS if self.cnt[s] > 0]


def _j_first(j):
    return j == 0


def set_tokens(a, b):
    return (a or []) + (b or [])


def _j_set(j):
    return j


def build_program(upto="full"):
    nc = bass.Bass("TRN2", target_bir_lowering=False)
    kb = KB()

    x0_d = nc.dram_tensor("x0", [NQT * 128, D], F32, kind="ExternalInput").ap()
    wqa_d = nc.dram_tensor("wqa", [D, LQ], F32, kind="ExternalInput").ap()
    qnw_d = nc.dram_tensor("qnw", [1, LQ], F32, kind="ExternalInput").ap()
    wqb_d = nc.dram_tensor("wqb", [LQ, H * DQK], F32, kind="ExternalInput").ap()
    wkr_d = nc.dram_tensor("wkr", [D, DR], F32, kind="ExternalInput").ap()
    wkva_d = nc.dram_tensor("wkva", [D, LKV], F32, kind="ExternalInput").ap()
    kvnw_d = nc.dram_tensor("kvnw", [1, LKV], F32, kind="ExternalInput").ap()
    wkvb_d = nc.dram_tensor("wkvb", [LKV, H * 256], F32, kind="ExternalInput").ap()
    wo_d = nc.dram_tensor("wo", [D, D], F32, kind="ExternalInput").ap()
    fw_d = nc.dram_tensor("fw", [D, D], F32, kind="ExternalInput").ap()
    fb_d = nc.dram_tensor("fb", [1, D], F32, kind="ExternalInput").ap()
    eye_d = nc.dram_tensor("eye", [128, 128], F32, kind="ExternalInput").ap()
    mask_d = nc.dram_tensor("maskc", [128, 512], F32, kind="ExternalInput").ap()
    cqt_d = nc.dram_tensor("cqt", [32, 1024], F32, kind="ExternalInput").ap()
    sqt_d = nc.dram_tensor("sqt", [32, 1024], F32, kind="ExternalInput").ap()
    y_d = nc.dram_tensor("y", [NQT * 128, D], F32, kind="ExternalOutput").ap()
    x1s_d = nc.dram_tensor("x1s", [NQT * 128, D], F32, kind="Internal").ap()
    cex_i = nc.dram_tensor("cex_i", [128, 4 * 1024], BF16, kind="Internal").ap()
    cex_o = nc.dram_tensor("cex_o", [4 * 128, 4 * 1024], BF16, kind="Internal").ap()
    krx_i = nc.dram_tensor("krx_i", [64, 1024], BF16, kind="Internal").ap()
    krx_o = nc.dram_tensor("krx_o", [4 * 64, 1024], BF16, kind="Internal").ap()
    x2s_d = nc.dram_tensor("x2s", [NQT * 128, D], F32, kind="Internal").ap()
    wts_d = nc.dram_tensor("wts", [D, D], BF16, kind="Internal").ap()
    o_d = nc.dram_tensor("o_scr", [128, H * 1024], BF16, kind="Internal").ap()

    glob = contextlib.ExitStack()

    def sbg(name, shape, dt):
        return glob.enter_context(nc.sbuf_tensor(name, shape, dt))

    cols = sbg("cols", [128, 32], F32)
    rows = sbg("rows", [1, 512], F32)
    m_cols = sbg("m_cols", [128, 8], F32)
    ones_f = sbg("ones_f", [1, 128], F32)
    ones_c = sbg("ones_c", [128, 1], F32)
    id_b = sbg("id_b", [128, 128], BF16)
    hb = sbg("hb", [128, D], BF16)

    def C(i, n=1):
        return cols[:, i:i + n]

    psS1 = glob.enter_context(nc.psum_tensor("psS1", [128, 1024], F32))
    psS2 = glob.enter_context(nc.psum_tensor("psS2", [128, 1024], F32))
    psA = psS1[:, 0:512]
    psB = psS1[:, 512:1024]
    psC = glob.enter_context(nc.psum_tensor("psC", [128, 512], F32))
    psD = glob.enter_context(nc.psum_tensor("psD", [128, 512], F32))
    psE = glob.enter_context(nc.psum_tensor("psE", [128, 512], F32))
    psF = glob.enter_context(nc.psum_tensor("psF", [128, 512], F32))
    ps_pool = [psA, psB]
    ps_o = [psC, psD]
    ps_gate = {0: [], 1: []}

    t_id = kb.dma("gpsimd", id_b[:, :], eye_d)
    t_m1 = kb.op("vector", lambda e: e.memset(ones_f[:, :], 1.0))
    t_m2 = kb.op("vector", lambda e: e.memset(ones_c[:, :], 1.0))
    t_m3 = kb.op("vector", lambda e: e.memset(C(ACC), 0.0))

    stA = contextlib.ExitStack()
    w_tT = stA.enter_context(nc.sbuf_tensor("w_tT", [128, 16 * D], BF16))
    p0 = contextlib.ExitStack()
    wtile = [p0.enter_context(nc.sbuf_tensor(f"wtile{i}", [128, D], F32))
             for i in range(4)]
    gtl2 = [p0.enter_context(nc.sbuf_tensor(f"gtl{i}", [128, D], F32))
            for i in range(2)]
    wt_tm = [p0.enter_context(nc.sbuf_tensor(f"wt_tm{i}", [128, D], BF16))
             for i in range(2)]

    rd_tok = {}
    for oc in range(16):
        w = wtile[oc % 4]
        ld = kb.dma("scalar", w[:, :], fw_d[oc * 128:(oc + 1) * 128, :],
                    waits=[rd_tok.get(oc - 4)])
        kb.op("vector", lambda e, _w=w: e.reduce_sum(
            out=C(TMP), in_=_w[:, :], axis=AX, apply_absolute_value=True), waits=[ld])
        rd_tok[oc] = kb.op("vector", lambda e: e.tensor_tensor(
            out=C(ACC), in0=C(ACC), in1=C(TMP), op=ADD), waits=[t_m3])
    t_tot = kb.op("tensor", lambda e: e.matmul(
        psE[0:1, 0:1], C(ACC), ones_c[:, :], start=True, stop=True),
        waits=[rd_tok[15], t_m2, t_m1])
    kb.op("vector", lambda e: e.tensor_scalar(
        out=rows[0:1, 0:1], in0=psE[0:1, 0:1], scalar1=0.5 / float(D * D),
        scalar2=0.5e-10, op0=MUL, op1=ADD), waits=[t_tot])
    t_rw = kb.op("vector", lambda e: e.tensor_scalar(
        out=rows[0:1, 1:2], in0=psE[0:1, 0:1], scalar1=0.5 / float(D * D),
        scalar2=None, op0=MUL))
    t_bc = kb.op("tensor", lambda e: e.matmul(
        psE[:, 2:4], ones_f[:, :], rows[0:1, 0:2], start=True, stop=True),
        waits=[t_rw, t_m1])
    kb.op("vector", lambda e: e.tensor_copy(out=C(TCOL), in_=psE[:, 2:3]), waits=[t_bc])
    t_cc = kb.op("vector", lambda e: e.tensor_copy(out=C(CCOL), in_=psE[:, 3:4]))
    t_nt = kb.op("vector", lambda e: e.tensor_scalar(
        out=C(NTCOL), in0=C(TCOL), scalar1=-1.0, scalar2=None, op0=MUL))

    qz_tok = {}
    trl_tok = {}
    cp_hist = []
    for oc in range(16):
        w = wtile[oc % 4]
        ld = kb.dma("scalar", w[:, :], fw_d[oc * 128:(oc + 1) * 128, :],
                    waits=[qz_tok.get(oc - 4, t_nt)])
        wt = wt_tm[oc % 2]
        gtl = gtl2[oc % 2]
        kb.op("vector", lambda e, _w=w, _g=gtl: e.tensor_scalar(
            out=_g[:, :], in0=_w[:, :], scalar1=C(NTCOL), scalar2=None, op0=LT),
            waits=[ld, t_nt, qz_tok.get(oc - 2)], relax=True)
        tq2 = kb.op("vector", lambda e, _w=w, _wt=wt, _g=gtl: e.scalar_tensor_tensor(
            out=_wt[:, :], in0=_w[:, :], scalar=C(TCOL), in1=_g[:, :], op0=GT,
            op1=SUB), waits=[trl_tok.get(oc - 2)])
        qz_tok[oc] = tq2
        trl_tok[oc] = kb.dma(
            "scalar",
            w_tT[:, :].rearrange("p (ic f) -> p ic f", ic=16)[:, :,
                                                              oc * 128:(oc + 1) * 128],
            wt[:, :].rearrange("p (ic c) -> p ic c", ic=16),
            waits=[tq2], transpose=True)
    wtT_snap = kb.now("t0", "t1", "t2", "t3")
    t_wtT_ready = wtT_snap[0]
    for ic in range(16):
        kb.dma("gpsimd", wts_d[ic * 128:(ic + 1) * 128, :],
               w_tT[:, ic * D:(ic + 1) * D], waits=wtT_snap)

    prev = None
    for g in range(4):
        ldb = kb.dma("sync", rows[0:1, 0:512], fb_d[0:1, g * 512:(g + 1) * 512],
                     waits=[prev] + ([t_bc, t_rw] if g == 0 else []))
        th = kb.op("vector", lambda e: e.tensor_scalar(
            out=rows[0:1, 0:512], in0=rows[0:1, 0:512], scalar1=0.5, scalar2=None,
            op0=MUL), waits=[ldb])
        tb = kb.op("tensor", lambda e: e.matmul(
            psE[:, 0:512], ones_f[:, :], rows[0:1, 0:512], start=True, stop=True),
            waits=[th])
        prev = kb.op("vector", lambda e, _g=g: e.tensor_copy(
            out=hb[:, _g * 512:(_g + 1) * 512], in_=psE[:, 0:512]), waits=[tb])
    t_hb = prev

    def ffn_phase(stk, name, src_fn, src_tok, dst_fn, ntiles, out_sem, wtT_buf,
                  ready, load_ready=()):
        x0t = [stk.enter_context(nc.sbuf_tensor(f"{name}x{i}", [128, D], F32))
               for i in range(2)]
        stage2 = [stk.enter_context(nc.sbuf_tensor(f"{name}st{i}", [128, D], F32))
                  for i in range(2)]
        z_tm2 = [stk.enter_context(nc.sbuf_tensor(f"{name}zt{i}", [128, D], BF16))
                 for i in range(2)]
        z_fm2 = [stk.enter_context(nc.sbuf_tensor(f"{name}zf{i}", [128, 16 * 128],
                                                  BF16)) for i in range(2)]
        x1t2 = [stk.enter_context(nc.sbuf_tensor(f"{name}o{i}", [128, D], F32))
                for i in range(2)]
        last_res = {}
        last_sq = {}
        last_store = {}
        last_mm = {}
        last_tr = {}
        stores = []

        def emit_ln(j):
            xb = x0t[j % 2]
            stage = stage2[j % 2]
            z_tm = z_tm2[j % 2]
            z_fm = z_fm2[j % 2]
            lw = [last_res.get(j - 2), last_sq.get(j - 2), src_tok(j)]
            if j < 2:
                lw += list(load_ready)
            ld = kb.dma("sync", xb[:, :], src_fn(j), waits=lw)
            kb.op("vector", lambda e, _x=xb: e.reduce_sum(
                out=C(SUM), in_=_x[:, :], axis=AX), waits=[ld])
            ts_sq = kb.op("scalar", lambda e, _x=xb, _st=stage: e.activation(
                out=_st[:, :], in_=_x[:, :], func=AF.Square, accum_out=C(SQ)),
                waits=[ld, last_res.get(j - 2)])
            last_sq[j] = ts_sq
            kb.op("vector", lambda e: e.tensor_scalar(
                out=C(MEAN), in0=C(SUM), scalar1=1.0 / D, scalar2=None, op0=MUL))
            kb.op("vector", lambda e: e.tensor_scalar(
                out=C(M2), in0=C(MEAN), scalar1=C(MEAN), scalar2=None, op0=MUL))
            kb.op("vector", lambda e: e.tensor_scalar(
                out=C(VAR), in0=C(SQ), scalar1=1.0 / D, scalar2=EPS_LN,
                op0=MUL, op1=ADD), waits=[ts_sq])
            t_var = kb.op("vector", lambda e: e.tensor_tensor(
                out=C(VAR), in0=C(VAR), in1=C(M2), op=SUB))
            t_sd = kb.op("scalar", lambda e: e.activation(
                out=C(SD), in_=C(VAR), func=AF.Sqrt), waits=[t_var])
            kb.op("vector", lambda e: e.reciprocal(out=C(RSTD), in_=C(SD)),
                  waits=[t_sd])
            t_rc = kb.op("vector", lambda e: e.tensor_scalar(
                out=C(RC), in0=C(RSTD), scalar1=C(CCOL), scalar2=None, op0=MUL))
            kb.op("vector", lambda e, _x=xb, _st=stage: e.tensor_scalar(
                out=_st[:, :], in0=_x[:, :], scalar1=C(MEAN), scalar2=None, op0=SUB),
                waits=[ts_sq])
            t_z = kb.op("vector", lambda e, _zt=z_tm, _st=stage: e.tensor_scalar(
                out=_zt[:, :], in0=_st[:, :], scalar1=C(RC), scalar2=None, op0=MUL),
                waits=[t_rc] + list(last_tr.get(j - 2) or []))
            t_ztr = kb.dma("scalar",
                           z_fm[:, :].rearrange("p (t c) -> p t c", t=16),
                           z_tm[:, :].rearrange("p (t c) -> p t c", t=16),
                           waits=[t_z, last_mm.get(j - 2)], transpose=True)
            last_tr[j] = [t_ztr]

        emit_ln(0)
        for t in range(ntiles):
            xb = x0t[t % 2]
            stage = stage2[t % 2]
            z_fm = z_fm2[t % 2]
            x1t = x1t2[t % 2]
            zfm_snap = last_tr[t]
            for og in range(4):
                par = og % 2
                ps = ps_pool[par]
                mw = list(zfm_snap) + list(ready) + ps_gate[par]
                tok_mm = None
                for ic in range(16):
                    tok_mm = kb.op("tensor", lambda e, _ic=ic, _og=og, _ps=ps,
                                   _zf=z_fm: e.matmul(
                        _ps[:, :], _zf[:, _ic * 128:(_ic + 1) * 128],
                        wtT_buf[:, _ic * D + _og * 512: _ic * D + (_og + 1) * 512],
                        start=(_ic == 0), stop=(_ic == 15)),
                        waits=mw if ic == 0 else [])
                last_mm[t] = tok_mm
                ew = [tok_mm, t_hb]
                if og == 0:
                    ew.append(last_store.get(t - 2))
                t_e1 = kb.op("vector", lambda e, _og=og, _ps=ps,
                             _st=stage: e.scalar_tensor_tensor(
                    out=_st[:, _og * 512:(_og + 1) * 512], in0=_ps[:, :], scalar=1.0,
                    in1=hb[:, _og * 512:(_og + 1) * 512], op0=MUL, op1=ADD), waits=ew,
                    relax=(og > 0))
                ps_gate[par] = [t_e1]
                t_e2 = kb.op("vector", lambda e, _og=og, _x=xb, _st=stage,
                             _xo=x1t: e.tensor_tensor(
                    out=_xo[:, _og * 512:(_og + 1) * 512],
                    in0=_st[:, _og * 512:(_og + 1) * 512],
                    in1=_x[:, _og * 512:(_og + 1) * 512], op=ADD))
                last_res[t] = t_e2
                if og == 0 and t + 1 < ntiles:
                    emit_ln(t + 1)
            st = kb.dma("scalar", dst_fn(t), x1t[:, :], waits=[last_res[t]],
                        sem=out_sem)
            last_store[t] = st
            stores.append(st)
        return stores

    if upto == "p0":
        dbg = kb.dma("gpsimd", y_d[0:128, :], hb[:, :], waits=[t_hb, t_wtT_ready],
                     sem="outd")
        kb.raw("sync", lambda e: e.wait_ge(kb.sems["outd"], 16))
        return _assemble(nc, kb, glob)
    p1a = contextlib.ExitStack()
    p1a_stores = ffn_phase(
        p1a, "fa", lambda t: x0_d[t * 128:(t + 1) * 128, :], lambda t: None,
        lambda t: x1s_d[t * 128:(t + 1) * 128, :], NQT, None,
        w_tT, wtT_snap + [t_cc, t_hb])
    bar1 = kb.now_all()
    if upto == "p1a":
        kb.dma("sync", y_d[0:128, :], x1s_d[0:128, :], waits=bar1, sem="outd")
        kb.raw("sync", lambda e: e.wait_ge(kb.sems["outd"], 16))
        return _assemble(nc, kb, glob)
    p1a.close()
    p0.close()
    stA.close()

    res = contextlib.ExitStack()
    ckv_fm = res.enter_context(nc.sbuf_tensor("ckv_fm", [128, 4 * T], BF16))
    krot_fm = res.enter_context(nc.sbuf_tensor("krot_fm", [128, T], BF16))
    cq_fm = res.enter_context(nc.sbuf_tensor("cq_fm", [128, 12 * 1024], BF16))

    p1b = contextlib.ExitStack()
    wqa_s = p1b.enter_context(nc.sbuf_tensor("wqa_s", [128, 16 * LQ], BF16))
    wkva_s = p1b.enter_context(nc.sbuf_tensor("wkva_s", [128, 16 * LKV], BF16))
    wkr_s = p1b.enter_context(nc.sbuf_tensor("wkr_s", [128, 16 * DR], BF16))
    x1r = [p1b.enter_context(nc.sbuf_tensor(f"x1r{i}", [128, D], F32))
           for i in range(2)]
    stage1 = p1b.enter_context(nc.sbuf_tensor("stage1", [128, D], F32))
    h_tm2 = [p1b.enter_context(nc.sbuf_tensor(f"h_tm{i}", [128, D], BF16))
             for i in range(2)]
    h_fm2 = [p1b.enter_context(nc.sbuf_tensor(f"h_fm{i}", [128, 16 * 128], BF16))
             for i in range(2)]
    cqs = p1b.enter_context(nc.sbuf_tensor("cqs", [128, LQ], F32))
    cqn_tm = p1b.enter_context(nc.sbuf_tensor("cqn_tm", [128, LQ], BF16))
    ckvn_tm = p1b.enter_context(nc.sbuf_tensor("ckvn_tm", [128, LKV], BF16))
    wq_t = p1b.enter_context(nc.sbuf_tensor("wq_t", [128, LQ], F32))
    wkv_t = p1b.enter_context(nc.sbuf_tensor("wkv_t", [128, LKV], F32))
    cksk = p1b.enter_context(nc.sbuf_tensor("cksk", [64, 1024], BF16))
    ckv_own = p1b.enter_context(nc.sbuf_tensor("ckv_own", [128, 4 * 1024], BF16))
    krot_own = p1b.enter_context(nc.sbuf_tensor("krot_own", [64, 1024], BF16))

    wl_last = None
    for c0 in range(16):
        kb.dma("gpsimd", wqa_s[:, c0 * LQ:(c0 + 1) * LQ],
               wqa_d[c0 * 128:(c0 + 1) * 128, :], waits=bar1 if c0 == 0 else [])
        kb.dma("gpsimd", wkva_s[:, c0 * LKV:(c0 + 1) * LKV],
               wkva_d[c0 * 128:(c0 + 1) * 128, :])
        wl_last = kb.dma("gpsimd", wkr_s[:, c0 * DR:(c0 + 1) * DR],
                         wkr_d[c0 * 128:(c0 + 1) * 128, :])
    kb.dma("gpsimd", cksk[0:32, :], cqt_d)
    kb.dma("gpsimd", cksk[32:64, :], sqt_d)
    wl_toks = kb.now("g0", "g1", "g2", "g3")

    prev = None
    for g in range(3):
        ldq = kb.dma("sync", rows[0:1, 0:512], qnw_d[0:1, g * 512:(g + 1) * 512],
                     waits=bar1 + [prev])
        tb = kb.op("tensor", lambda e: e.matmul(
            psE[:, 0:512], ones_f[:, :], rows[0:1, 0:512], start=True, stop=True),
            waits=[ldq] + bar1)
        prev = kb.op("vector", lambda e, _g=g: e.tensor_copy(
            out=wq_t[:, _g * 512:(_g + 1) * 512], in_=psE[:, 0:512]),
            waits=[tb] + bar1)
    ldk = kb.dma("sync", rows[0:1, 0:512], kvnw_d[0:1, 0:512], waits=[prev])
    tb = kb.op("tensor", lambda e: e.matmul(
        psE[:, 0:512], ones_f[:, :], rows[0:1, 0:512], start=True, stop=True),
        waits=[ldk])
    t_wnorm = kb.op("vector", lambda e: e.tensor_copy(
        out=wkv_t[:, 0:512], in_=psE[:, 0:512]), waits=[tb])

    if upto == "p1bw":
        kb.dma("gpsimd", y_d[0:128, 0:D], wqa_s[:, 0:D], waits=kb.now_all(),
               sem="outd")
        kb.raw("sync", lambda e: e.wait_ge(kb.sems["outd"], 16))
        return _assemble(nc, kb, glob)

    lnh_tok = {}
    lnsq_tok = {}
    htr_tok = {}
    hread_tok = {}

    def emit_ln(j):
        b = j % 2
        xb = x1r[b]
        lw = [lnh_tok.get(j - 2), lnsq_tok.get(j - 2), p1a_stores[j]]
        if j < 2:
            lw += bar1
        ld = kb.dma("sync", xb[:, :], x1s_d[j * 128:(j + 1) * 128, :], waits=lw)
        kb.op("vector", lambda e, _x=xb: e.reduce_sum(
            out=C(SUM), in_=_x[:, :], axis=AX), waits=[ld])
        t_nm = kb.op("vector", lambda e: e.tensor_scalar(
            out=C(MEAN), in0=C(SUM), scalar1=-1.0 / D, scalar2=None, op0=MUL))
        ts_sq = kb.op("scalar", lambda e, _x=xb, _ht=h_tm2[b]: e.activation(
            out=_ht[:, :], in_=_x[:, :], func=AF.Square, bias=C(MEAN),
            accum_out=C(SQ)), waits=[ld, t_nm, htr_tok.get(j - 2)])
        lnsq_tok[j] = ts_sq
        t_var = kb.op("vector", lambda e: e.tensor_scalar(
            out=C(VAR), in0=C(SQ), scalar1=1.0 / D, scalar2=EPS_LN,
            op0=MUL, op1=ADD), waits=[ts_sq])
        t_sd = kb.op("scalar", lambda e: e.activation(
            out=C(VAR), in_=C(VAR), func=AF.Sqrt), waits=[t_var])
        t_rstd = kb.op("vector", lambda e: e.reciprocal(out=C(RSTD), in_=C(VAR)),
                       waits=[t_sd])
        kb.op("vector", lambda e, _x=xb: e.tensor_scalar(
            out=_x[:, :], in0=_x[:, :], scalar1=C(MEAN), scalar2=None, op0=ADD))
        t_h = kb.op("vector", lambda e, _x=xb, _ht=h_tm2[b]: e.tensor_scalar(
            out=_ht[:, :], in0=_x[:, :], scalar1=C(RSTD), scalar2=None, op0=MUL))
        lnh_tok[j] = t_h
        hread_tok[j - 2] = kb.now("te")[0]
        htr_tok[j] = kb.dma("scalar",
                            h_fm2[b][:, :].rearrange("p (t c) -> p t c", t=16),
                            h_tm2[b][:, :].rearrange("p (t c) -> p t c", t=16),
                            waits=[t_h, hread_tok.get(j - 2)], transpose=True)

    emit_ln(0)
    last_ckvtr = []
    cqtr = []
    for t in range(NQT):
        own = True
        i_own = t
        if t + 1 < NQT:
            emit_ln(t + 1)
        h_fm = h_fm2[t % 2]
        hfm_snap = [htr_tok[t]]

        mw = list(hfm_snap) + wl_toks + ps_gate[0]
        tok_mm = None
        for ic in range(16):
            tok_mm = kb.op("tensor", lambda e, _ic=ic, _hf=h_fm: e.matmul(
                psA[:, :], _hf[:, _ic * 128:(_ic + 1) * 128],
                wkva_s[:, _ic * LKV:(_ic + 1) * LKV],
                start=(_ic == 0), stop=(_ic == 15)), waits=mw if ic == 0 else [])
        t_kvc = kb.op("vector", lambda e: e.tensor_copy(
            out=stage1[:, 0:512], in_=psA[:, :]), waits=[tok_mm])
        ps_gate[0] = [t_kvc]
        kb.op("vector", lambda e: e.tensor_tensor(
            out=stage1[:, 512:1024], in0=stage1[:, 0:512], in1=stage1[:, 0:512],
            op=MUL))
        t_sqkv = kb.op("vector", lambda e: e.reduce_sum(
            out=C(SQ2), in_=stage1[:, 512:1024], axis=AX))
        t_ms = kb.op("vector", lambda e: e.tensor_scalar(
            out=C(SD), in0=C(SQ2), scalar1=1.0 / LKV, scalar2=EPS_RMS,
            op0=MUL, op1=ADD), waits=[t_sqkv])
        t_sd2 = kb.op("scalar", lambda e: e.activation(
            out=C(SD), in_=C(SD), func=AF.Sqrt), waits=[t_ms])
        t_rr = kb.op("vector", lambda e: e.reciprocal(out=C(RC), in_=C(SD)),
                     waits=[t_sd2])
        t_n1 = kb.op("vector", lambda e: e.tensor_scalar(
            out=stage1[:, 0:512], in0=stage1[:, 0:512], scalar1=C(RC), scalar2=None,
            op0=MUL), waits=[t_rr])
        t_ckv = kb.op("vector", lambda e: e.tensor_tensor(
            out=ckvn_tm[:, :], in0=stage1[:, 0:512], in1=wkv_t[:, :], op=MUL),
            waits=[t_wnorm] + last_ckvtr)
        t_ckvtr = kb.dma(
            "scalar",
            ckv_own[:, :].rearrange("p (ci f) -> p ci f", ci=4)[:, :,
                                                               t * 128:(t + 1) * 128],
            ckvn_tm[:, :].rearrange("p (ci c) -> p ci c", ci=4),
            waits=[t_ckv] + last_ckvtr, transpose=True)
        last_ckvtr = [t_ckvtr]

        mw = list(hfm_snap) + wl_toks + ps_gate[1]
        tok_kr = None
        for ic in range(16):
            tok_kr = kb.op("tensor", lambda e, _ic=ic, _hf=h_fm: e.matmul(
                psB[0:64, 0:128], wkr_s[:, _ic * DR:(_ic + 1) * DR],
                _hf[:, _ic * 128:(_ic + 1) * 128],
                start=(_ic == 0), stop=(_ic == 15)), waits=mw if ic == 0 else [])
        ks = slice(t * 128, (t + 1) * 128)
        kb.op("vector", lambda e, _ks=ks: e.tensor_tensor(
            out=stage1[0:32, 0:128], in0=psB[0:32, 0:128], in1=cksk[0:32, _ks], op=MUL),
            waits=[tok_kr] + wl_toks)
        kb.op("vector", lambda e, _ks=ks: e.tensor_tensor(
            out=stage1[0:32, 128:256], in0=psB[32:64, 0:128], in1=cksk[32:64, _ks], op=MUL))
        kb.op("vector", lambda e, _ks=ks: e.tensor_tensor(
            out=krot_own[0:32, _ks], in0=stage1[0:32, 0:128],
            in1=stage1[0:32, 128:256], op=SUB))
        kb.op("vector", lambda e, _ks=ks: e.tensor_tensor(
            out=stage1[0:32, 0:128], in0=psB[32:64, 0:128], in1=cksk[0:32, _ks], op=MUL))
        t_r5 = kb.op("vector", lambda e, _ks=ks: e.tensor_tensor(
            out=stage1[0:32, 128:256], in0=psB[0:32, 0:128], in1=cksk[32:64, _ks], op=MUL))
        t_r6 = kb.op("vector", lambda e, _ks=ks: e.tensor_tensor(
            out=krot_own[32:64, _ks], in0=stage1[0:32, 0:128],
            in1=stage1[0:32, 128:256], op=ADD))
        ps_gate[1] = [t_r5]

        if own:
            skip_tr = upto.startswith("p1bq")
            cp_q = []
            for og in range(3):
                par = og % 2
                ps = ps_pool[par]
                mw = list(hfm_snap) + ps_gate[par]
                tok_q = None
                for ic in range(16):
                    tok_q = kb.op("tensor", lambda e, _ic=ic, _og=og, _ps=ps,
                                  _hf=h_fm: e.matmul(
                        _ps[:, :], _hf[:, _ic * 128:(_ic + 1) * 128],
                        wqa_s[:, _ic * LQ + _og * 512: _ic * LQ + (_og + 1) * 512],
                        start=(_ic == 0), stop=(_ic == 15)),
                        waits=mw if ic == 0 else [])
                t_cp = kb.op("vector", lambda e, _og=og, _ps=ps: e.tensor_copy(
                    out=cqs[:, _og * 512:(_og + 1) * 512], in_=_ps[:, :]),
                    waits=[tok_q])
                ps_gate[par] = [t_cp]
                cp_q.append(t_cp)
            kb.op("vector", lambda e: e.tensor_tensor(
                out=stage1[:, 512:512 + LQ], in0=cqs[:, :], in1=cqs[:, :], op=MUL),
                waits=[cp_q[2]])
            kb.op("vector", lambda e: e.reduce_sum(
                out=C(SQ2), in_=stage1[:, 512:512 + LQ], axis=AX))
            t_msq = kb.op("vector", lambda e: e.tensor_scalar(
                out=C(SD), in0=C(SQ2), scalar1=1.0 / LQ, scalar2=EPS_RMS,
                op0=MUL, op1=ADD))
            t_sdq = kb.op("scalar", lambda e: e.activation(
                out=C(SD), in_=C(SD), func=AF.Sqrt), waits=[t_msq])
            t_rq = kb.op("vector", lambda e: e.reciprocal(out=C(RC), in_=C(SD)),
                         waits=[t_sdq])
            t_cqn = None
            for og in range(3):
                kb.op("vector", lambda e, _og=og: e.tensor_scalar(
                    out=stage1[:, _og * 512:(_og + 1) * 512],
                    in0=cqs[:, _og * 512:(_og + 1) * 512],
                    scalar1=C(RC), scalar2=None, op0=MUL),
                    waits=[t_rq, cp_q[og]])
                t_cqn = kb.op("vector", lambda e, _og=og: e.tensor_tensor(
                    out=cqn_tm[:, _og * 512:(_og + 1) * 512],
                    in0=stage1[:, _og * 512:(_og + 1) * 512],
                    in1=wq_t[:, _og * 512:(_og + 1) * 512], op=MUL),
                    waits=[t_wnorm] + (cqtr if og == 0 else []))
            if not skip_tr:
                cqtr = [kb.dma(
                    "scalar",
                    cq_fm[:, :].rearrange("p (ci f) -> p ci f", ci=12)[
                        :, :, i_own * 128:(i_own + 1) * 128],
                    cqn_tm[:, :].rearrange("p (ci c) -> p ci c", ci=12),
                    waits=[t_cqn], transpose=True)]
        if upto.startswith("p1b") and upto[3:].isdigit() and t == int(upto[3:]) - 1:
            break
        if upto.startswith("p1bq") and t == 3:
            break

    exw = kb.now("ve", "se", "t0", "t1", "t2", "t3")
    st_ckv = kb.dma("scalar", cex_i, ckv_own[:, :], waits=exw)
    st_kr = kb.dma("scalar", krx_i, krot_own[:, :], waits=exw)
    rg = [[0, 1, 2, 3], [4, 5, 6, 7]]
    kb.raw("gpsimd", lambda e, _a=st_ckv, _b=st_kr: e.wait_ge(
        kb.sems[_a[0]], _a[1]))
    kb.raw("gpsimd", lambda e, _b=st_kr: e.wait_ge(kb.sems[_b[0]], _b[1]))
    kb.cnt["gc"] += 2

    def _coll(e, _rg=rg):
        e.collective_compute("AllGather", AluOpType.bypass, replica_groups=_rg,
                             ins=[cex_i], outs=[cex_o]).then_inc(kb.sems["gc"], 1)
        e.collective_compute("AllGather", AluOpType.bypass, replica_groups=_rg,
                             ins=[krx_i], outs=[krx_o]).then_inc(kb.sems["gc"], 1)
    kb.raw("gpsimd", _coll)
    coll_tok = ("gc", kb.cnt["gc"])
    gl = []
    for b in range(4):
        for ci in range(4):
            gl.append(kb.dma(
                "sync",
                ckv_fm[:, ci * T:(ci + 1) * T].rearrange(
                    "p (t f) -> p t f", f=512)[:, :, b * 128:(b + 1) * 128],
                cex_o[b * 128:(b + 1) * 128, ci * 1024:(ci + 1) * 1024].rearrange(
                    "p (t c) -> p t c", t=8),
                waits=[coll_tok]))
        gl.append(kb.dma(
            "sync",
            krot_fm[0:64, :].rearrange("p (t f) -> p t f", f=512)[
                :, :, b * 128:(b + 1) * 128],
            krx_o[b * 64:(b + 1) * 64, :].rearrange("p (t c) -> p t c", t=8),
            waits=[coll_tok]))
    gl_toks = kb.now("s0", "s1", "s2", "s3")
    kb.op("vector", lambda e: e.memset(krot_fm[64:128, :], 0.0))

    bar2 = kb.now_all()
    if upto.startswith("p1b") and upto != "p1bw":
        kb.dma("gpsimd", y_d[0:128, 0:D], ckv_fm[:, 0:D],
               waits=bar2, sem="outd")
        kb.raw("sync", lambda e: e.wait_ge(kb.sems["outd"], 16))
        return _assemble(nc, kb, glob)
    p1b.close()

    p2 = contextlib.ExitStack()
    wqb_s = [p2.enter_context(nc.sbuf_tensor(f"wqb{i}", [128, 12 * DQK], BF16))
             for i in range(2)]
    wkvb_s = [p2.enter_context(nc.sbuf_tensor(f"wkvb{i}", [128, 4 * 256], BF16))
              for i in range(2)]
    qfa = p2.enter_context(nc.sbuf_tensor("qfa", [128, 1024], BF16))
    qfb = p2.enter_context(nc.sbuf_tensor("qfb", [128, 1024], BF16))
    kp_fm = p2.enter_context(nc.sbuf_tensor("kp_fm", [128, T], BF16))
    v_tm = p2.enter_context(nc.sbuf_tensor("v_tm", [128, 32 * 128], BF16))
    S_2 = [p2.enter_context(nc.sbuf_tensor(f"S_{i}", [128, T], F32))
           for i in range(2)]
    P_s = [p2.enter_context(nc.sbuf_tensor(f"P{j}", [128, T], BF16)) for j in range(4)]
    PT_n = p2.enter_context(nc.sbuf_tensor("PT_n", [128, 4 * 2048], BF16))
    v_fm = p2.enter_context(nc.sbuf_tensor("v_fm", [128, 16 * 128], BF16))
    o_hd = [p2.enter_context(nc.sbuf_tensor(f"o_hd{i}", [128, 1024], BF16))
            for i in range(2)]
    rsumb = p2.enter_context(nc.sbuf_tensor("rsumb", [128, 512], F32))
    rs4 = p2.enter_context(nc.sbuf_tensor("rs4", [128, 8], F32))
    mask_b = p2.enter_context(nc.sbuf_tensor("mask_b", [128, 512], BF16))
    id_f = p2.enter_context(nc.sbuf_tensor("id_f", [128, 128], F32))
    cqsq = p2.enter_context(nc.sbuf_tensor("cqsq", [64, 1024], BF16))
    cqsq2 = p2.enter_context(nc.sbuf_tensor("cqsq2", [64, 1024], BF16))
    rope_t = p2.enter_context(nc.sbuf_tensor("rope_t", [64, 1024], F32))
    qfb_f = p2.enter_context(nc.sbuf_tensor("qfb_f", [64, 1024], F32))

    t_mk = kb.dma("gpsimd", mask_b[:, :], mask_d, waits=bar2)
    t_idf = kb.dma("sync", id_f[:, :], eye_d, waits=bar2)
    kb.op("vector", lambda e: e.memset(qfb[64:128, :], 0.0))
    t_cqs = kb.dma("gpsimd", cqsq[0:32, :], cqt_d)
    kb.dma("gpsimd", cqsq[32:64, :], sqt_d)
    kb.dma("gpsimd", cqsq2[32:64, :], cqt_d)
    t_qt = kb.dma("gpsimd", cqsq2[0:32, :], sqt_d)

    def load_head_w(h, waits):
        for ic in range(12):
            kb.dma("gpsimd", wqb_s[h % 2][:, ic * DQK:(ic + 1) * DQK],
                   wqb_d[ic * 128:(ic + 1) * 128, h * DQK:(h + 1) * DQK],
                   waits=waits if ic == 0 else [])
        for ci in range(4):
            kb.dma("gpsimd", wkvb_s[h % 2][:, ci * 256:(ci + 1) * 256],
                   wkvb_d[ci * 128:(ci + 1) * 128, h * 256:(h + 1) * 256])
        return kb.now("g0", "g1", "g2", "g3")

    head_wload = {0: load_head_w(0, bar2 + [t_qt])}
    head_te_end = {}
    exp_gate = {0: None, 1: None}
    oev_gate = {0: None, 1: None}
    pslot_gate = {j: None for j in range(4)}
    pt_gate = None
    s_gate = {0: [], 1: [], 2: [], 3: []}
    pv_prev_half = None
    t_ptz = kb.op("vector", lambda e: e.memset(PT_n[:, :], 0.0), waits=bar2)
    pt_zero = {j: t_ptz for j in range(4)}
    ost_gate = {(p, g): None for p in range(2) for g in range(2)}
    v_gate = []
    prev_rw4 = None
    prev_tr4 = None
    prev_rsb = None
    prev_oev = None

    for h in range(H):
        wq = wqb_s[h % 2]
        wkv = wkvb_s[h % 2]
        wld = head_wload[h]
        if h + 1 < H:
            head_wload[h + 1] = load_head_w(h + 1, [head_te_end.get(h - 1)])
        te_prev = head_te_end.get(h - 1)
        t_qfa = {}
        for qs in range(2):
            par = qs % 2
            ps = ps_pool[par]
            mw = wld + ps_gate[par]
            tok = None
            for ic in range(12):
                tok = kb.op("tensor", lambda e, _ic=ic, _qs=qs, _ps=ps, _wq=wq: e.matmul(
                    _ps[:, :], _wq[:, _ic * DQK:_ic * DQK + 128],
                    cq_fm[:, _ic * 1024 + _qs * 512: _ic * 1024 + (_qs + 1) * 512],
                    start=(_ic == 0), stop=(_ic == 11)),
                    waits=mw if ic == 0 else [])
            t_qfa[qs] = kb.op("vector", lambda e, _qs=qs, _ps=ps: e.tensor_copy(
                out=qfa[:, _qs * 512:(_qs + 1) * 512], in_=_ps[:, :]),
                waits=[tok, te_prev])
            ps_gate[par] = [t_qfa[qs]]
        t_qf_last = None
        for qs in range(2):
            par = qs % 2
            ps = ps_pool[par]
            mw = ps_gate[par]
            tok = None
            for ic in range(12):
                tok = kb.op("tensor", lambda e, _ic=ic, _qs=qs, _ps=ps, _wq=wq: e.matmul(
                    _ps[0:64, :], _wq[:, _ic * DQK + 128:(_ic + 1) * DQK],
                    cq_fm[:, _ic * 1024 + _qs * 512: _ic * 1024 + (_qs + 1) * 512],
                    start=(_ic == 0), stop=(_ic == 11)),
                    waits=mw if ic == 0 else [])
            sl = slice(qs * 512, (qs + 1) * 512)
            t_qe = kb.op("vector", lambda e, _ps=ps, _qs=qs: e.tensor_copy(
                out=qfb_f[:, _qs * 512:(_qs + 1) * 512], in_=_ps[0:64, :]),
                waits=[tok])
            ps_gate[par] = [t_qe]
        kb.op("vector", lambda e: e.tensor_tensor(
            out=qfb[0:32, :], in0=qfb_f[0:32, :], in1=cqsq[0:32, :], op=MUL),
            waits=[t_qt, t_cqs])
        kb.op("vector", lambda e: e.tensor_tensor(
            out=rope_t[0:32, :], in0=qfb_f[32:64, :], in1=cqsq[32:64, :], op=MUL))
        kb.op("vector", lambda e: e.tensor_tensor(
            out=qfb[0:32, :], in0=qfb[0:32, :], in1=rope_t[0:32, :], op=SUB))
        kb.op("vector", lambda e: e.tensor_tensor(
            out=qfb[32:64, :], in0=qfb_f[32:64, :], in1=cqsq2[32:64, :], op=MUL))
        kb.op("vector", lambda e: e.tensor_tensor(
            out=rope_t[32:64, :], in0=qfb_f[0:32, :], in1=cqsq2[0:32, :], op=MUL))
        t_qf_last = kb.op("vector", lambda e: e.tensor_tensor(
            out=qfb[32:64, :], in0=qfb[32:64, :], in1=rope_t[32:64, :], op=ADD))
        t_kp = {}
        for kc8 in range(8):
            par = kc8 % 2
            ps = ps_pool[par]
            mw = wld + ps_gate[par]
            tok = None
            for ci in range(4):
                tok = kb.op("tensor", lambda e, _ci=ci, _k=kc8, _ps=ps, _w=wkv: e.matmul(
                    _ps[:, :], _w[:, _ci * 256:_ci * 256 + 128],
                    ckv_fm[:, _ci * T + _k * 512: _ci * T + (_k + 1) * 512],
                    start=(_ci == 0), stop=(_ci == 3)),
                    waits=mw if ci == 0 else [])
            t_kp[kc8] = kb.op("vector", lambda e, _k=kc8, _ps=ps: e.tensor_copy(
                out=kp_fm[:, _k * 512:(_k + 1) * 512], in_=_ps[:, :]), waits=[tok],
                relax=(kc8 > 0))
            ps_gate[par] = [t_kp[kc8]]
        t_vfm = {}
        vtr = []
        for kc8 in range(8):
            par = kc8 % 2
            ps = ps_pool[par]
            mw = list(ps_gate[par])
            if kc8 == 4:
                mw.append(vtr[0])
            tok = None
            for ci in range(4):
                tok = kb.op("tensor", lambda e, _ci=ci, _k=kc8, _ps=ps, _w=wkv: e.matmul(
                    _ps[:, :], _w[:, _ci * 256 + 128:(_ci + 1) * 256],
                    ckv_fm[:, _ci * T + _k * 512: _ci * T + (_k + 1) * 512],
                    start=(_ci == 0), stop=(_ci == 3)),
                    waits=mw if ci == 0 else [])
            t_vfm[kc8] = kb.op("vector", lambda e, _k=kc8, _ps=ps: e.tensor_copy(
                out=v_fm[:, (_k % 4) * 512:(_k % 4 + 1) * 512], in_=_ps[:, :]),
                waits=[tok] + (v_gate if kc8 == 0 else [])
                + ([vtr[0]] if kc8 == 4 else []), relax=(kc8 % 4 != 0))
            ps_gate[par] = [t_vfm[kc8]]
            if kc8 == 3 or kc8 == 7:
                half = kc8 // 4
                vtr.append(kb.dma(
                    "scalar",
                    v_tm[:, half * 2048:(half + 1) * 2048].rearrange(
                        "p (t c) -> p t c", t=16),
                    v_fm[:, :].rearrange("p (t c) -> p t c", t=16),
                    waits=[t_vfm[kc8]] + ([te_prev] if half == 0 else []),
                    transpose=True))
        v_ready = [vtr[0], vtr[1]]
        v_gate = [vtr[1]]

        grp = {g: dict(msk=[None] * 4, ea=[None] * 4, pad=[None] * 4,
                       trw=[None] * 4, t_rs=None, nkt=2048 * (g + 1) // 128)
               for g in range(2)}
        tile_st = {}
        BIAS_COLS = [BIAS, ACC]

        def emit_A(g, j):
            s_gate[0] = list(set_tokens(s_gate[0], ps_gate[0]))
            s_gate[1] = list(set_tokens(s_gate[1], ps_gate[1]))
            i = 4 * g + j
            K = 512 * (i + 1)
            qsl = slice(i * 128, (i + 1) * 128)
            Sb = S_2[i % 2]
            sc_last = None
            npair = 0
            for kc8 in range(i + 1):
                slot = kc8 % 4
                psx = psS1 if slot < 2 else psS2
                reg = psx[:, (slot % 2) * 512:(slot % 2 + 1) * 512]
                mw = [t_kp[kc8], t_qf_last] + s_gate[slot]
                kb.op("tensor", lambda e, _k=kc8, _r=reg, _q=qsl: e.matmul(
                    _r, qfa[:, _q], kp_fm[:, _k * 512:(_k + 1) * 512],
                    start=True, stop=False), waits=mw)
                tok = kb.op("tensor", lambda e, _k=kc8, _r=reg, _q=qsl: e.matmul(
                    _r, qfb[:, _q], krot_fm[:, _k * 512:(_k + 1) * 512],
                    start=False, stop=True))
                if slot % 2 == 1 or kc8 == i:
                    w = 1024 if slot % 2 == 1 else 512
                    sc_last = kb.op(
                        "vector", lambda e, _k=kc8, _S=Sb, _w=w, _n=npair,
                        _px=psx: e.tensor_scalar(
                            out=_S[:, (_k + 1) * 512 - _w:(_k + 1) * 512],
                            in0=_px[:, 0:_w],
                            scalar1=1.0, scalar2=NEG_BIG, op0=MUL, op1=MAXOP,
                            accum_out=m_cols[:, _n:_n + 1]),
                        waits=[tok, exp_gate[i % 2]], relax=(npair > 0))
                    s_gate[slot - (slot % 2)] = [sc_last]
                    s_gate[slot - (slot % 2) + 1] = [sc_last]
                    npair += 1
            ps_gate[0] = s_gate[0]
            ps_gate[1] = s_gate[1]
            bc = BIAS_COLS[i % 2]
            t_bias = kb.op("vector", lambda e, _b=bc, _n=npair: e.tensor_scalar(
                out=m_cols[:, 0:_n], in0=m_cols[:, 0:_n], scalar1=-SCALE,
                scalar2=3.0e38, op0=MUL, op1=AluOpType.min,
                accum_out=C(_b)), waits=[sc_last, exp_gate[i % 2]])
            tile_st[(g, j)] = dict(K=K, Sb=Sb, bias=t_bias, bc=bc)

        prev_sum = [None]

        def emit_B(g, j):
            i = 4 * g + j
            st = tile_st[(g, j)]
            K, Sb, bc = st["K"], st["Sb"], st["bc"]
            maxK = 2048 * (g + 1)
            Pj = P_s[j]
            pw = [st["bias"], prev_sum[0]] + (pslot_gate[j] or [])
            t_ea = None
            if i > 0:
                t_ea = kb.op("scalar", lambda e, _P=Pj, _K=K, _S=Sb,
                             _b=bc: e.activation(
                    out=_P[:, 0:_K - 512], in_=_S[:, 0:_K - 512], func=AF.Exp,
                    bias=C(_b), scale=SCALE, accum_out=C(SA)), waits=pw)
            t_ed = kb.op("scalar", lambda e, _P=Pj, _K=K, _S=Sb, _b=bc: e.activation(
                out=_P[:, _K - 512:_K], in_=_S[:, _K - 512:_K], func=AF.Exp,
                bias=C(_b), scale=SCALE), waits=pw)
            exp_gate[i % 2] = t_ed
            t_msk = kb.op("vector", lambda e, _P=Pj, _K=K: e.scalar_tensor_tensor(
                out=_P[:, _K - 512:_K], in0=_P[:, _K - 512:_K], scalar=1.0,
                in1=mask_b[:, :], op0=MUL, op1=MUL, accum_out=C(SK)),
                waits=[t_ed, t_mk])
            grp[g]["msk"][j] = t_msk
            grp[g]["ea"][j] = t_ea
            if K < maxK:
                grp[g]["pad"][j] = kb.op("vector", lambda e, _P=Pj, _K=K, _M=maxK:
                                         e.memset(_P[:, _K:_M], 0.0),
                                         waits=pslot_gate[j] or [])
            if i > 0:
                t_sum = kb.op("vector", lambda e: e.tensor_tensor(
                    out=C(SSUM), in0=C(SK), in1=C(SA), op=ADD), waits=[t_ea])
            else:
                t_sum = kb.op("vector", lambda e: e.tensor_copy(
                    out=C(SSUM), in_=C(SK)))
            prev_sum[0] = t_sum
            grp[g]["t_rs"] = kb.op("vector", lambda e, _c=i: e.reciprocal(
                out=rs4[:, _c:_c + 1], in_=C(SSUM)),
                waits=[prev_tr4] if i == 0 else [])
            if g == 0:
                used = (j + 1) * 4
                grp[0]["trw"][j] = kb.dma(
                    "scalar",
                    PT_n[:, j * 2048: j * 2048 + used * 128].rearrange(
                        "p (t c) -> p t c", t=used),
                    P_s[j][:, 0:used * 128].rearrange("p (t c) -> p t c", t=used),
                    waits=[t_msk, t_ea, grp[0]["pad"][j], pt_gate,
                           pt_zero[j]], transpose=True)
                pslot_gate[j] = [grp[0]["trw"][j]]

        tiles = [(g, j) for g in range(2) for j in range(4)]
        emit_A(0, 0)
        for idx in range(1, 8):
            emit_A(*tiles[idx])
            emit_B(*tiles[idx - 1])
        emit_B(*tiles[7])

        for g in range(2):
            st = grp[g]
            nkt = st["nkt"]
            po = ps_o[g]
            pv_tok = {}
            for half in range(nkt // 16):
                if g == 1:
                    trw = []
                    if half == 1:
                        trw_h0 = list(pslot_gate[0] + pslot_gate[1]
                                      + pslot_gate[2] + pslot_gate[3])
                        for j in range(3):
                            pt_zero[j] = kb.op(
                                "vector", lambda e, _j=j: e.memset(
                                    PT_n[:, _j * 2048 + (_j + 1) * 512:
                                         (_j + 1) * 2048], 0.0),
                                waits=([pt_gate] + trw_h0)
                                if _j_first(j) else [])
                    for j in range(4):
                        used = 16 if half == 0 else (j + 1) * 4
                        tw = [st["msk"][j], st["ea"][j], st["pad"][j], pt_gate]
                        if half == 1:
                            tw.append(pt_zero[j])
                        trw.append(kb.dma(
                            "scalar",
                            PT_n[:, j * 2048: j * 2048 + used * 128].rearrange(
                                "p (t c) -> p t c", t=used),
                            P_s[j][:, half * 2048:
                                   half * 2048 + used * 128].rearrange(
                                "p (t c) -> p t c", t=used),
                            waits=tw, transpose=True))
                        pslot_gate[j] = [trw[-1]]
                else:
                    trw = st["trw"]
                for kk in range(16):
                    kt = half * 16 + kk
                    rhs = PT_n[:, :].rearrange("p (j t c) -> p j t c",
                                               j=4, t=16)[:, :, kk, :]
                    pvw = list(trw) if kk == 0 else []
                    if kk == 0:
                        pvw += [pt_zero[j2] for j2 in range(4)]
                    if kt == 0:
                        pvw += [oev_gate[g]] + list(v_ready)
                    pv_tok[kt] = kb.op("tensor", lambda e, _kt=kt, _po=po, _r=rhs,
                                       _n=nkt: e.matmul(
                        _po[:, 0:512], v_tm[:, _kt * 128:(_kt + 1) * 128], _r,
                        start=(_kt == 0), stop=(_kt == _n - 1)),
                        waits=pvw)
                pt_gate = pv_tok[half * 16 + 15]
                pv_prev_half = pt_gate
            t_tr4 = None
            for j in range(4):
                t_tr4 = kb.op("tensor", lambda e, _j=j, _c=g * 4 + j: e.transpose(
                    psF[0:1, _j * 128:(_j + 1) * 128], rs4[:, _c:_c + 1],
                    id_f[:, :]),
                    waits=[st["t_rs"], t_idf] + ([prev_rw4] if j == 0 else []))
            prev_tr4 = t_tr4
            t_rw4 = kb.op("vector", lambda e: e.tensor_copy(
                out=rows[0:1, 0:512], in_=psF[0:1, 0:512]), waits=[t_tr4])
            prev_rw4 = t_rw4
            t_bc = None
            for j in range(4):
                t_bc = kb.op("tensor", lambda e, _j=j: e.matmul(
                    psE[:, _j * 128:(_j + 1) * 128], ones_f[:, :],
                    rows[0:1, _j * 128:(_j + 1) * 128], start=True, stop=True),
                    waits=[t_rw4] + ([prev_rsb] if j == 0 else []))
            t_rsb = kb.op("vector", lambda e: e.tensor_copy(
                out=rsumb[:, :], in_=psE[:, 0:512]), waits=[t_bc, prev_oev])
            prev_rsb = t_rsb
            t_oev = kb.op("vector", lambda e, _g=g, _po=po, _ob=o_hd[h % 2]:
                          e.tensor_tensor(
                out=_ob[:, _g * 512:(_g + 1) * 512],
                in0=_po[:, 0:512], in1=rsumb[:, :], op=MUL),
                waits=[t_rsb, pv_tok[nkt - 1], ost_gate[(h % 2, g)]])
            ost_gate[(h % 2, g)] = kb.dma(
                "sync", o_d[:, h * 1024 + g * 512: h * 1024 + (g + 1) * 512],
                o_hd[h % 2][:, g * 512:(g + 1) * 512], waits=[t_oev])
            oev_gate[g] = t_oev
            prev_oev = t_oev
        head_te_end[h] = kb.now("te")[0]

    bar3 = kb.now_all()
    if upto == "p2":
        kb.dma("gpsimd", y_d[0:128, 0:D], v_tm[:, 0:D],
               waits=bar3, sem="outd")
        kb.raw("sync", lambda e: e.wait_ge(kb.sems["outd"], 16))
        return _assemble(nc, kb, glob)
    p2.close()

    p3a = contextlib.ExitStack()
    wo_s = p3a.enter_context(nc.sbuf_tensor("wo_s", [128, 16 * D], BF16))
    x1o = [p3a.enter_context(nc.sbuf_tensor(f"x1o{i}", [128, D], F32))
           for i in range(2)]
    o_t = [p3a.enter_context(nc.sbuf_tensor(f"o_t{i}", [128, 16 * 128], BF16))
           for i in range(2)]
    x2t = [p3a.enter_context(nc.sbuf_tensor(f"x2t{i}", [128, D], F32))
           for i in range(2)]
    for ic in range(16):
        kb.dma("gpsimd", wo_s[:, ic * D:(ic + 1) * D],
               wo_d[ic * 128:(ic + 1) * 128, :],
               waits=bar3 if ic == 0 else [])
    wol_toks = kb.now("g0", "g1", "g2", "g3")
    last_r3 = {}
    last_st3 = {}
    last_omm = {}
    p3a_stores = []
    for i in range(NQT):
        xb = x1o[i % 2]
        ob = o_t[i % 2]
        x2 = x2t[i % 2]
        lw = [last_r3.get(i - 2), p1a_stores[i]]
        if i < 2:
            lw += bar3
        ld = kb.dma("sync", xb[:, :], x1s_d[i * 128:(i + 1) * 128, :], waits=lw)
        ldo = kb.dma("sync",
                     ob[:, :].rearrange("p (hc c) -> p hc c", hc=16),
                     o_d[:, :].rearrange("p (hc f) -> p hc f", hc=16)[
                         :, :, i * 128:(i + 1) * 128],
                     waits=[last_omm.get(i - 2)] + (bar3 if i < 2 else []))
        for og in range(4):
            par = og % 2
            ps = ps_pool[par]
            mw = [ld, ldo] + wol_toks + ps_gate[par]
            tok = None
            for hc in range(16):
                tok = kb.op("tensor", lambda e, _hc=hc, _og=og, _ps=ps,
                            _ob=ob: e.matmul(
                    _ps[:, :], _ob[:, _hc * 128:(_hc + 1) * 128],
                    wo_s[:, _hc * D + _og * 512: _hc * D + (_og + 1) * 512],
                    start=(_hc == 0), stop=(_hc == 15)),
                    waits=mw if hc == 0 else [])
            ew = [tok]
            if og == 0:
                ew.append(last_st3.get(i - 2))
            t_e = kb.op("vector", lambda e, _og=og, _ps=ps, _x=xb,
                        _x2=x2: e.tensor_tensor(
                out=_x2[:, _og * 512:(_og + 1) * 512], in0=_ps[:, :],
                in1=_x[:, _og * 512:(_og + 1) * 512], op=ADD), waits=ew)
            ps_gate[par] = [t_e]
            last_r3[i] = t_e
        last_omm[i] = tok
        last_st3[i] = kb.dma("scalar", x2s_d[i * 128:(i + 1) * 128, :], x2[:, :],
                             waits=[last_r3[i]])
        p3a_stores.append(last_st3[i])
    bar4 = kb.now_all()
    if upto == "p3a":
        kb.dma("sync", y_d[0:128, :], x2s_d[0:128, :], waits=bar4, sem="outd")
        kb.raw("sync", lambda e: e.wait_ge(kb.sems["outd"], 16))
        return _assemble(nc, kb, glob)
    p3a.close()
    res.close()

    p3b = contextlib.ExitStack()
    w_tT2 = p3b.enter_context(nc.sbuf_tensor("w_tT2", [128, 16 * D], BF16))
    for ic in range(16):
        kb.dma("sync", w_tT2[:, ic * D:(ic + 1) * D],
               wts_d[ic * 128:(ic + 1) * 128, :],
               waits=bar4 if ic == 0 else [])
    wrl_toks = kb.now("s0", "s1", "s2", "s3")
    ffn_phase(p3b, "fb", lambda t: x2s_d[t * 128:(t + 1) * 128, :],
              lambda t: p3a_stores[t],
              lambda t: y_d[t * 128:(t + 1) * 128, :], NQT, "outd",
              w_tT2, wrl_toks + bar4, load_ready=bar4)
    p3b.close()

    n_out = kb.cnt["outd"]
    kb.raw("sync", lambda e, _n=n_out: e.wait_ge(kb.sems["outd"], _n))
    return _assemble(nc, kb, glob)


def _assemble(nc, kb, glob):
    with contextlib.ExitStack() as semstk:
        for s in KB.ALL_SEMS:
            kb.sems[s] = semstk.enter_context(nc.semaphore(s))
        with nc.Block() as block:
            @block.tensor
            def _(eng):
                for f in kb.ops["tensor"]:
                    f(eng)

            @block.vector
            def _(eng):
                for f in kb.ops["vector"]:
                    f(eng)

            @block.scalar
            def _(eng):
                for f in kb.ops["scalar"]:
                    f(eng)

            @block.sync
            def _(eng):
                for f in kb.ops["sync"]:
                    f(eng)

            @block.gpsimd
            def _(eng):
                for f in kb.ops["gpsimd"]:
                    f(eng)
    return nc, kb


def make_in_maps(d):
    return _make_in_maps(**d)


def kernel(L0_attn_hidden_states, L0_attn_Wq_a, L0_attn_q_norm_w, L0_attn_Wq_b,
           L0_attn_Wk_rope, L0_attn_Wkv_a, L0_attn_kv_norm_w, L0_attn_Wkv_b,
           L0_attn_Wo, L0_ffn_weight, L0_ffn_bias):
    in_maps = _make_in_maps(L0_attn_hidden_states, L0_attn_Wq_a, L0_attn_q_norm_w,
                            L0_attn_Wq_b, L0_attn_Wk_rope, L0_attn_Wkv_a,
                            L0_attn_kv_norm_w, L0_attn_Wkv_b, L0_attn_Wo,
                            L0_ffn_weight, L0_ffn_bias)
    nc, _ = build_program()
    res = run_bass_kernel_spmd(nc, in_maps, core_ids=list(range(8)))

    out = np.empty((2, T, D), dtype=np.float32)
    for c in range(8):
        b, r = c // 4, c % 4
        yc = res.results[c]["y"]
        for i in range(NQT):
            g = 4 * i + r
            out[b, g * 128:(g + 1) * 128, :] = yc[i * 128:(i + 1) * 128, :]
    return out


def _make_in_maps(L0_attn_hidden_states, L0_attn_Wq_a, L0_attn_q_norm_w, L0_attn_Wq_b,
                  L0_attn_Wk_rope, L0_attn_Wkv_a, L0_attn_kv_norm_w, L0_attn_Wkv_b,
                  L0_attn_Wo, L0_ffn_weight, L0_ffn_bias):
    x = np.ascontiguousarray(L0_attn_hidden_states, dtype=np.float32)
    weights = dict(
        wqa=np.ascontiguousarray(L0_attn_Wq_a, dtype=np.float32),
        qnw=np.ascontiguousarray(L0_attn_q_norm_w, dtype=np.float32).reshape(1, LQ),
        wqb=np.ascontiguousarray(L0_attn_Wq_b, dtype=np.float32),
        wkr=np.ascontiguousarray(L0_attn_Wk_rope, dtype=np.float32),
        wkva=np.ascontiguousarray(L0_attn_Wkv_a, dtype=np.float32),
        kvnw=np.ascontiguousarray(L0_attn_kv_norm_w, dtype=np.float32).reshape(1, LKV),
        wkvb=np.ascontiguousarray(L0_attn_Wkv_b, dtype=np.float32),
        wo=np.ascontiguousarray(L0_attn_Wo, dtype=np.float32),
        fw=np.ascontiguousarray(L0_ffn_weight, dtype=np.float32),
        fb=np.ascontiguousarray(L0_ffn_bias, dtype=np.float32).reshape(1, D),
        eye=np.eye(128, dtype=np.float32),
    )
    inv_freq = 1.0 / (10000.0 ** (np.arange(0, DR, 2, dtype=np.float64) / DR))

    in_maps = []
    for c in range(8):
        b, r = c // 4, c % 4
        own_idx = ((4 * np.arange(NQT)[:, None] + r) * 128
                   + np.arange(128)[None, :]).reshape(-1)
        x0c = np.ascontiguousarray(x[b].reshape(T, D)[own_idx])
        own_rows = own_idx.astype(np.float64)
        cqt = np.cos(inv_freq[:, None] * own_rows[None, :]).astype(np.float32)
        sqt = np.sin(inv_freq[:, None] * own_rows[None, :]).astype(np.float32)
        mask = np.zeros((128, 512), dtype=np.float32)
        for p in range(4):
            if p < r:
                mask[:, p * 128:(p + 1) * 128] = 1.0
            elif p == r:
                mask[:, p * 128:(p + 1) * 128] = np.tril(
                    np.ones((128, 128), dtype=np.float32))
        m = dict(weights)
        m.update(x0=x0c, maskc=mask, cqt=cqt, sqt=sqt)
        in_maps.append(m)
    return in_maps


if __name__ == "__main__":
    nc, kb = build_program()
    print("instruction counts:",
          {k: len(v) for k, v in kb.ops.items()},
          "sem counts:", kb.cnt)
